# version 03
# baseline (speedup 1.0000x reference)
import numpy as np
from contextlib import ExitStack

import concourse.bass as bass
import concourse.mybir as mybir
from concourse.bass_utils import run_bass_kernel_spmd

F32 = mybir.dt.float32
BF16 = mybir.dt.bfloat16
AF = mybir.ActivationFunctionType
ALU = mybir.AluOpType
AX = mybir.AxisListType.X

T = 1024
SEQ = 4096
DM = 2048
C16 = DM // 128
H = 4
EPS = 1e-5
NEXP = 8
HID = 1024
RD = 0.08838834764831845
MODW = 12288
GROUPS = [[0, 1, 2, 3], [4, 5, 6, 7]]


def build_program(dbg=()):
    nc = bass.Bass("TRN2", target_bir_lowering=False, detect_race_conditions=False)

    x_tm = nc.dram_tensor("x_tm", [T, DM], F32, kind="ExternalInput")
    cfm = nc.dram_tensor("cfm", [128, C16], F32, kind="ExternalInput")
    wmod = nc.dram_tensor("wmod", [DM, MODW], F32, kind="ExternalInput")
    wqT = nc.dram_tensor("wqT", [DM, H * 128], F32, kind="ExternalInput")
    wkT = nc.dram_tensor("wkT", [DM, H * 128], F32, kind="ExternalInput")
    wvT = nc.dram_tensor("wvT", [DM, H * 128], F32, kind="ExternalInput")
    woT = nc.dram_tensor("woT", [DM, DM], F32, kind="ExternalInput")
    gammac = nc.dram_tensor("gammac", [128, 1], F32, kind="ExternalInput")
    betac = nc.dram_tensor("betac", [128, 1], F32, kind="ExternalInput")
    gatewT = nc.dram_tensor("gatewT", [DM, NEXP], F32, kind="ExternalInput")
    w1T = nc.dram_tensor("w1T", [NEXP, DM, HID], F32, kind="ExternalInput")
    w3T = nc.dram_tensor("w3T", [NEXP, DM, HID], F32, kind="ExternalInput")
    w2T = nc.dram_tensor("w2T", [NEXP, HID, DM], F32, kind="ExternalInput")
    sw1T = nc.dram_tensor("sw1T", [DM, HID], F32, kind="ExternalInput")
    sw3T = nc.dram_tensor("sw3T", [DM, HID], F32, kind="ExternalInput")
    sw2T = nc.dram_tensor("sw2T", [HID, DM], F32, kind="ExternalInput")
    identd = nc.dram_tensor("identd", [128, 128], F32, kind="ExternalInput")
    allonesd = nc.dram_tensor("allonesd", [128, 128], F32, kind="ExternalInput")
    masksd = nc.dram_tensor("masksd", [4 * 128, 512], F32, kind="ExternalInput")

    out = nc.dram_tensor("out", [DM, T], F32, kind="ExternalOutput")

    h1d = nc.dram_tensor("h1d", [DM, T], BF16)
    hg = nc.dram_tensor("hg", [4 * DM, T], BF16)
    od = nc.dram_tensor("od", [H * 128, SEQ], BF16)
    og = nc.dram_tensor("og", [DM, SEQ], BF16)
    xfmd = nc.dram_tensor("xfmd", [DM, T], F32)
    x2d = nc.dram_tensor("x2d", [DM, T], F32)

    dbg_outs = {}
    def mkdbg(name, shape, dt=F32):
        dbg_outs[name] = nc.dram_tensor("dbg_" + name, shape, dt, kind="ExternalOutput")
        return dbg_outs[name]

    root = ExitStack()
    with root:
        R = root.enter_context
        ident = R(nc.sbuf_tensor("ident", [128, 128], F32))
        allones = R(nc.sbuf_tensor("allones", [128, 128], BF16))
        gam = R(nc.sbuf_tensor("gam", [128, 1], F32))
        bet = R(nc.sbuf_tensor("bet", [128, 1], F32))
        modcols = R(nc.sbuf_tensor("modcols", [128, 96], F32))
        mca1 = R(nc.sbuf_tensor("mca1", [128, C16], F32))
        mcb1 = R(nc.sbuf_tensor("mcb1", [128, C16], F32))
        mga = R(nc.sbuf_tensor("mga", [128, C16], F32))
        mgb = R(nc.sbuf_tensor("mgb", [128, C16], F32))

        S = {}
        for name in ["mw0", "mw1", "hg0", "hg1", "xt0", "xt1", "xf0", "xf1",
                     "x2s0", "x2s1", "ww0", "ww1", "ww2", "xc0", "xc1",
                     "ou0", "ou1", "d_h1", "d_xfm", "d_od", "d_omy",
                     "d_c", "d_x", "d_hg", "d_st", "c_h", "c_o",
                     "m_pe", "m_ev", "m_tr", "m_trev", "m_dve",
                     "b2_sq", "b2_red", "b2_sd", "b2_row", "b2_ln", "b2_tr",
                     "b2_ev", "b2_free",
                     "q4_pe", "q4_u", "q4_u2", "q4_st", "q4_stf", "q4_rowA",
                     "q4_sd", "q4_rowb", "q4_bc", "q4_qt", "q4_v", "q4_free",
                     "a_s", "a_sq", "a_mk", "a_oz", "a_on",
                     "w7_pe", "w7_x2", "w7_xb", "w7_xsq", "w7_st",
                     "g2_row", "g2_sd", "g2_rowb", "g2_bc", "g2_t", "g2_h2",
                     "g_pe", "g_ev", "g_sig", "g_am", "g_a", "t8_tr", "t8_ev",
                     "d_ar", "d_w8", "cl_a", "cl_v", "d_mask",
                     "e_pu", "e_done", "e_silu", "e_tA", "e_yy", "e_w",
                     "e_A", "e_Aev",
                     "d_x2", "f_dve", "d_out"]:
            S[name] = R(nc.semaphore(name))
        DC = [0]; DX = [0]; DST = [0]

        scf = R(nc.sbuf_tensor("scf", [128, 2 * C16], F32))
        cst = R(nc.sbuf_tensor("cst", [128, 2], F32))
        scb = R(nc.sbuf_tensor("scb", [128, C16], BF16))

        with nc.Block() as blk:
            @blk.gpsimd
            def _(g):
                g.dma_start(ident[:, :], identd[:, :]).then_inc(S["d_c"], 16)
                g.dma_start(allones[:, :], allonesd[:, :]).then_inc(S["d_c"], 16)
                g.dma_start(gam[:, :], gammac[:, :]).then_inc(S["d_c"], 16)
                g.dma_start(bet[:, :], betac[:, :]).then_inc(S["d_c"], 16)
                g.dma_start(scf[:, 0:C16], cfm[:, :]).then_inc(S["d_c"], 16)
                g.memset(cst[:, 0:1], EPS)
                g.memset(cst[:, 1:2], -0.5)
                g.wait_ge(S["d_c"], 16 * 5)
        DC[0] = 16 * 5

        NW = MODW // 512
        with ExitStack() as ph:
            E = ph.enter_context
            siluc = E(nc.sbuf_tensor("siluc", [128, C16], F32))
            wmt = [E(nc.sbuf_tensor(f"wmt{i}", [128, C16 * 512], BF16)) for i in range(2)]
            modrow = E(nc.sbuf_tensor("modrow", [1, MODW], F32))
            pmod = [E(nc.psum_tensor(f"pmod{i}", [1, 512], F32)) for i in range(2)]
            ptr1 = [E(nc.psum_tensor(f"ptr1{i}", [128, 1], F32)) for i in range(4)]

            with nc.Block() as blk:
                @blk.scalar
                def _(s):
                    s.activation(siluc[:, :], scf[:, 0:C16], AF.Silu)
                    s.drain().then_inc(S["m_ev"], 1)

                @blk.vector
                def _(v):
                    v.wait_ge(S["m_ev"], 1)
                    v.tensor_copy(scb[:, :], siluc[:, :])
                    v.drain().then_inc(S["m_dve"], 1)

                @blk.gpsimd
                def _(g):
                    for nw in range(NW):
                        if nw >= 2:
                            g.wait_ge(S["m_pe"], nw - 1)
                        g.dma_start(
                            wmt[nw % 2][:, :].rearrange("p (c n) -> p c n", c=C16),
                            wmod.ap().rearrange("(c p) m -> p c m", c=C16)[:, :, nw*512:(nw+1)*512],
                        ).then_inc(S["mw%d" % (nw % 2)], 16)

                @blk.tensor
                def _(t):
                    t.wait_ge(S["m_dve"], 1)
                    for nw in range(NW):
                        t.wait_ge(S["mw%d" % (nw % 2)], 16 * (nw // 2 + 1))
                        if nw >= 2:
                            t.wait_ge(S["m_ev"], nw)
                        for c in range(C16):
                            ins = t.matmul(pmod[nw % 2][:, :], scb[:, c:c+1],
                                           wmt[nw % 2][:, c*512:(c+1)*512],
                                           start=(c == 0), stop=(c == C16 - 1))
                        ins.then_inc(S["m_pe"], 1)

                @blk.scalar
                def _(s):
                    for nw in range(NW):
                        s.wait_ge(S["m_pe"], nw + 1)
                        s.copy(modrow[0:1, nw*512:(nw+1)*512], pmod[nw % 2][:, :]
                               ).then_inc(S["m_ev"], 1)

            with nc.Block() as blk:
                @blk.tensor
                def _(t):
                    for k in range(96):
                        if k >= 4:
                            t.wait_ge(S["m_trev"], k - 3)
                        t.transpose(ptr1[k % 4][:, :], modrow[0:1, k*128:(k+1)*128],
                                    ident[0:1, 0:1]).then_inc(S["m_tr"], 1)

                @blk.scalar
                def _(s):
                    for k in range(96):
                        s.wait_ge(S["m_tr"], k + 1)
                        s.copy(modcols[:, k:k+1], ptr1[k % 4][:, :]).then_inc(S["m_trev"], 1)

            with nc.Block() as blk:
                @blk.scalar
                def _(s):
                    for n, (dst, off) in enumerate([(mga, 32), (mgb, 80)]):
                        if n >= 1:
                            s.wait_ge(S["cl_v"], n)
                        s.activation(scf[:, 0:16], modcols[:, off:off+16], AF.Abs)
                        s.drain().then_inc(S["cl_a"], 1)

                @blk.vector
                def _(v):
                    v.tensor_scalar_add(mca1[:, :], modcols[:, 16:32], 1.0)
                    v.tensor_scalar_add(mcb1[:, :], modcols[:, 64:80], 1.0)
                    for n, (dst, off) in enumerate([(mga, 32), (mgb, 80)]):
                        gsl = modcols[:, off:off+16]
                        v.tensor_single_scalar(dst[:, :], gsl, 0.0, ALU.is_ge)
                        v.drain()
                        v.tensor_scalar(dst[:, :], dst[:, :], 2.0, -1.0, ALU.mult, ALU.add)
                        v.wait_ge(S["cl_a"], n + 1)
                        v.tensor_single_scalar(scf[:, 16:32], scf[:, 0:16], 0.25, ALU.max)
                        v.drain()
                        v.tensor_mul(dst[:, :], dst[:, :], scf[:, 16:32])
                        v.drain().then_inc(S["cl_v"], 1)

            if "mod" in dbg:
                dmod = mkdbg("mod", [128, 96])
                with nc.Block() as blk:
                    @blk.sync
                    def _(sy):
                        sy.dma_start(dmod[:, :], modcols[:, :]).then_inc(S["d_st"], 16)
                        sy.wait_ge(S["d_st"], DST[0] + 16)
                DST[0] += 16

        with ExitStack() as ph:
            E = ph.enter_context
            h1_sb = E(nc.sbuf_tensor("h1_sb", [128, C16 * T], BF16))
            xfm_sb = E(nc.sbuf_tensor("xfm_sb", [128, C16 * T], F32))
            xt = [E(nc.sbuf_tensor(f"xt{i}", [128, DM], F32)) for i in range(2)]
            scr = [E(nc.sbuf_tensor(f"scr{i}", [128, DM], F32)) for i in range(2)]
            sxx = [E(nc.sbuf_tensor(f"sxx{i}", [128, 1], F32)) for i in range(2)]
            sx = [E(nc.sbuf_tensor(f"sx{i}", [128, 1], F32)) for i in range(2)]
            mut = [E(nc.sbuf_tensor(f"mut{i}", [128, 1], F32)) for i in range(2)]
            vart = [E(nc.sbuf_tensor(f"vart{i}", [128, 1], F32)) for i in range(2)]
            sdt = [E(nc.sbuf_tensor(f"sdt{i}", [128, 1], F32)) for i in range(2)]
            rstdt = [E(nc.sbuf_tensor(f"rstdt{i}", [128, 1], F32)) for i in range(2)]
            nmt = [E(nc.sbuf_tensor(f"nmt{i}", [128, 1], F32)) for i in range(2)]
            lnb = [E(nc.sbuf_tensor(f"lnb{i}", [128, DM], F32)) for i in range(2)]
            dbg01 = E(nc.sbuf_tensor("dbg01", [128, 8], F32))
            pt = [E(nc.psum_tensor(f"pt{i}", [128, 128], F32)) for i in range(4)]

            NT = T // 128
            with nc.Block() as blk:
                @blk.sync
                def _(sy):
                    for i in range(NT):
                        if i >= 2:
                            sy.wait_ge(S["b2_ev"], 32 * (i - 1))
                        sy.dma_start(xt[i % 2][:, :], x_tm[i*128:(i+1)*128, :]
                                     ).then_inc(S["xt%d" % (i % 2)], 16)

                @blk.scalar
                def _(s):
                    for i in range(NT):
                        s.wait_ge(S["xt%d" % (i % 2)], 16 * (i // 2 + 1))
                        if i >= 2:
                            s.wait_ge(S["b2_red"], 2 * (i - 2) + 1)
                        s.activation(scr[i % 2][:, :], xt[i % 2][:, :], AF.Square
                                     ).then_inc(S["b2_sq"], 1)
                        s.wait_ge(S["b2_red"], 2 * i + 1)
                        s.activation(sdt[i % 2][:, :], vart[i % 2][:, :], AF.Ln,
                                     bias=cst[:, 0:1])
                        s.drain()
                        s.activation(rstdt[i % 2][:, :], sdt[i % 2][:, :], AF.Exp,
                                     scale=cst[:, 1:2])
                        s.drain().then_inc(S["b2_sd"], 1)
                        for k in range(32):
                            s.wait_ge(S["b2_tr"], 32 * i + k + 1)
                            c = k // 2
                            bank = pt[k % 4]
                            if k % 2 == 0:
                                s.copy(xfm_sb[:, c*T + i*128:c*T + (i+1)*128], bank[:, :]
                                       ).then_inc(S["b2_ev"], 1)
                            else:
                                s.activation(h1_sb[:, c*T + i*128:c*T + (i+1)*128],
                                             bank[:, :], AF.Identity,
                                             scale=mca1[:, c:c+1],
                                             bias=modcols[:, c:c+1]
                                             ).then_inc(S["b2_ev"], 1)

                @blk.vector
                def _(v):
                    for i in range(NT):
                        v.wait_ge(S["xt%d" % (i % 2)], 16 * (i // 2 + 1))
                        if i >= 2:
                            v.wait_ge(S["b2_ln"], i - 1)
                        v.reduce_sum(sx[i % 2][:, :], xt[i % 2][:, :], axis=AX)
                        v.wait_ge(S["b2_sq"], i + 1)
                        v.reduce_sum(sxx[i % 2][:, :], scr[i % 2][:, :], axis=AX)
                        v.drain()
                        v.tensor_single_scalar(mut[i % 2][:, :], sx[i % 2][:, :],
                                               1.0 / DM, ALU.mult)
                        v.tensor_single_scalar(vart[i % 2][:, :], sxx[i % 2][:, :],
                                               1.0 / DM, ALU.mult)
                        v.drain()
                        v.tensor_mul(nmt[i % 2][:, :], mut[i % 2][:, :], mut[i % 2][:, :])
                        v.drain()
                        v.tensor_sub(vart[i % 2][:, :], vart[i % 2][:, :], nmt[i % 2][:, :])
                        v.drain().then_inc(S["b2_red"], 2)
                        v.wait_ge(S["b2_sd"], i + 1)
                        v.tensor_mul(nmt[i % 2][:, :], mut[i % 2][:, :], rstdt[i % 2][:, :])
                        v.drain()
                        v.tensor_single_scalar(nmt[i % 2][:, :], nmt[i % 2][:, :],
                                               -1.0, ALU.mult)
                        v.drain()
                        if dbg and i < 2:
                            v.tensor_copy(dbg01[:, 4*i+0:4*i+1], vart[i % 2][:, :])
                            v.tensor_copy(dbg01[:, 4*i+1:4*i+2], sdt[i % 2][:, :])
                            v.tensor_copy(dbg01[:, 4*i+2:4*i+3], rstdt[i % 2][:, :])
                            v.tensor_copy(dbg01[:, 4*i+3:4*i+4], nmt[i % 2][:, :])
                        v.tensor_scalar(lnb[i % 2][:, :], xt[i % 2][:, :],
                                        rstdt[i % 2][:, :], nmt[i % 2][:, :],
                                        ALU.mult, ALU.add).then_inc(S["b2_ln"], 1)

                @blk.tensor
                def _(t):
                    for i in range(NT):
                        t.wait_ge(S["xt%d" % (i % 2)], 16 * (i // 2 + 1))
                        t.wait_ge(S["b2_ln"], i + 1)
                        for k in range(32):
                            if 32 * i + k >= 4:
                                t.wait_ge(S["b2_ev"], 32 * i + k - 3)
                            c = k // 2
                            src = xt[i % 2] if k % 2 == 0 else lnb[i % 2]
                            t.transpose(pt[k % 4][:, :], src[:, c*128:(c+1)*128],
                                        ident[:, :]).then_inc(S["b2_tr"], 1)
            with nc.Block() as blk:
                @blk.sync
                def _(sy):
                    sy.dma_start(h1d.ap().rearrange("(c p) t -> p c t", c=C16),
                                 h1_sb[:, :].rearrange("p (c t) -> p c t", c=C16)
                                 ).then_inc(S["d_h1"], 16)
                    sy.dma_start(xfmd.ap().rearrange("(c p) t -> p c t", c=C16),
                                 xfm_sb[:, :].rearrange("p (c t) -> p c t", c=C16)
                                 ).then_inc(S["d_xfm"], 16)
                    sy.wait_ge(S["d_xfm"], 16)

                @blk.gpsimd
                def _(g):
                    g.wait_ge(S["d_h1"], 16)
                    g.collective_compute(
                        "AllGather", ALU.bypass, replica_groups=GROUPS,
                        ins=[h1d.ap().opt()], outs=[hg.ap().opt()],
                    ).then_inc(S["c_h"], 1)

            if "b2s" in dbg:
                d01 = mkdbg("t01", [128, 8])
                with nc.Block() as blk:
                    @blk.sync
                    def _(sy):
                        sy.dma_start(d01[:, :], dbg01[:, :]).then_inc(S["d_st"], 16)
                        sy.wait_ge(S["d_st"], DST[0] + 16)
                DST[0] += 16
                dst2 = mkdbg("b2s", [128, 12])
                with nc.Block() as blk:
                    @blk.sync
                    def _(sy):
                        for bi, tt in enumerate([sx, sxx, mut, vart, sdt, rstdt]):
                            for par in range(2):
                                with nc.allow_non_contiguous_dma("dbg"):
                                    sy.dma_start(dst2[:, 2*bi+par:2*bi+par+1], tt[par][:, :],
                                                 allow_slow_non_contiguous=True
                                                 ).then_inc(S["d_st"], 16)
                        sy.wait_ge(S["d_st"], DST[0] + 16 * 12)
                DST[0] += 16 * 12

            if "h1" in dbg:
                dh1 = mkdbg("h1", [DM, T]); dxf = mkdbg("xfm", [DM, T])
                with nc.Block() as blk:
                    @blk.gpsimd
                    def _(sy):
                        sy.wait_ge(S["d_h1"], 16)
                        sy.wait_ge(S["d_xfm"], 16)
                        sy.dma_start(dh1.ap(), h1d.ap()).then_inc(S["d_st"], 16)
                        sy.dma_start(dxf.ap(), xfmd.ap()).then_inc(S["d_st"], 16)
                        sy.wait_ge(S["d_st"], DST[0] + 32)
                DST[0] += 32

        s2 = ExitStack()
        E2 = s2.enter_context
        Q_sb = E2(nc.sbuf_tensor("Q_sb", [128, H * SEQ], BF16))
        K_sb = E2(nc.sbuf_tensor("K_sb", [128, H * SEQ], BF16))
        V_sb = E2(nc.sbuf_tensor("V_sb", [128, H * SEQ], BF16))

        with ExitStack() as ph:
            E = ph.enter_context
            wq_bf = E(nc.sbuf_tensor("wq_bf", [128, C16 * 512], BF16))
            wk_bf = E(nc.sbuf_tensor("wk_bf", [128, C16 * 512], BF16))
            wv_bf = E(nc.sbuf_tensor("wv_bf", [128, C16 * 512], BF16))
            hgt = [E(nc.sbuf_tensor(f"hgt{i}", [128, C16 * 512], BF16)) for i in range(2)]
            u_q = [E(nc.sbuf_tensor(f"u_q{i}", [128, 512], BF16)) for i in range(4)]
            u_k = [E(nc.sbuf_tensor(f"u_k{i}", [128, 512], BF16)) for i in range(4)]
            u2q = [E(nc.sbuf_tensor(f"u2q{i}", [128, 512], BF16)) for i in range(4)]
            u2k = [E(nc.sbuf_tensor(f"u2k{i}", [128, 512], BF16)) for i in range(4)]
            rows = {nm: E(nc.sbuf_tensor("r_" + nm, [1, 512], F32))
                    for nm in ["mur", "var", "sd", "rr", "rs", "mrs"]}
            rowsb = {nm: E(nc.sbuf_tensor("rb_" + nm, [1, 512], BF16))
                    for nm in ["rs", "mrs"]}
            pq = E(nc.psum_tensor("pq", [128, 512], F32))
            pk = E(nc.psum_tensor("pk", [128, 512], F32))
            pv = E(nc.psum_tensor("pv", [128, 512], F32))
            pst1 = E(nc.psum_tensor("pst1", [1, 512], F32))
            pst2 = E(nc.psum_tensor("pst2", [1, 512], F32))
            pbc1 = E(nc.psum_tensor("pbc1", [128, 512], F32))
            pbc2 = E(nc.psum_tensor("pbc2", [128, 512], F32))

            with nc.Block() as blk:
                @blk.gpsimd
                def _(g):
                    for (dst, src) in [(wq_bf, wqT), (wk_bf, wkT), (wv_bf, wvT)]:
                        g.dma_start(dst[:, :].rearrange("p (c i) -> p c i", c=C16),
                                    src.ap().rearrange("(c p) i -> p c i", c=C16)
                                    ).then_inc(S["d_c"], 16)

                @blk.sync
                def _(sy):
                    sy.wait_ge(S["c_h"], 1)
                    for rw in range(8):
                        if rw >= 2:
                            sy.wait_ge(S["q4_pe"], 12 * (rw - 2) + 12)
                        r, win = rw // 2, rw % 2
                        sy.dma_start(
                            hgt[rw % 2][:, :].rearrange("p (c t) -> p c t", c=C16),
                            hg[r*DM:(r+1)*DM, win*512:(win+1)*512]
                              .rearrange("(c p) t -> p c t", c=C16),
                        ).then_inc(S["hg%d" % (rw % 2)], 16)

                @blk.tensor
                def _(t):
                    def stats_bcast(g):
                        t.wait_ge(S["q4_u2"], 2 * g + 1)
                        if g >= 1:
                            t.wait_ge(S["q4_stf"], 2 * g)
                        t.matmul(pst1[:, :], allones[:, 0:1], u_q[g % 4][:, :],
                                 start=True, stop=True, skip_group_check=True)
                        t.matmul(pst2[:, :], allones[:, 0:1], u2q[g % 4][:, :],
                                 start=True, stop=True, skip_group_check=True
                                 ).then_inc(S["q4_st"], 1)
                        t.wait_ge(S["q4_rowb"], 2 * g + 1)
                        if g >= 1:
                            t.wait_ge(S["q4_qt"], 2 * g)
                        t.matmul(pbc1[:, :], allones[0:1, :], rowsb["rs"][:, :],
                                 start=True, stop=True, skip_group_check=True)
                        t.matmul(pbc2[:, :], allones[0:1, :], rowsb["mrs"][:, :],
                                 start=True, stop=True, skip_group_check=True
                                 ).then_inc(S["q4_bc"], 1)
                        t.wait_ge(S["q4_u2"], 2 * g + 2)
                        t.wait_ge(S["q4_stf"], 2 * g + 1)
                        t.matmul(pst1[:, :], allones[:, 0:1], u_k[g % 4][:, :],
                                 start=True, stop=True, skip_group_check=True)
                        t.matmul(pst2[:, :], allones[:, 0:1], u2k[g % 4][:, :],
                                 start=True, stop=True, skip_group_check=True
                                 ).then_inc(S["q4_st"], 1)
                        t.wait_ge(S["q4_rowb"], 2 * g + 2)
                        t.wait_ge(S["q4_qt"], 2 * g + 1)
                        t.matmul(pbc1[:, :], allones[0:1, :], rowsb["rs"][:, :],
                                 start=True, stop=True, skip_group_check=True)
                        t.matmul(pbc2[:, :], allones[0:1, :], rowsb["mrs"][:, :],
                                 start=True, stop=True, skip_group_check=True
                                 ).then_inc(S["q4_bc"], 1)

                    t.wait_ge(S["d_c"], DC[0] + 48)
                    for rw in range(8):
                        t.wait_ge(S["hg%d" % (rw % 2)], 16 * (rw // 2 + 1))
                        for h in range(H):
                            g = 4 * rw + h
                            if g >= 1:
                                t.wait_ge(S["q4_u"], 2 * (g - 1) + 1)
                            for c in range(C16):
                                ins = t.matmul(pq[:, :],
                                               wq_bf[:, c*512 + h*128: c*512 + (h+1)*128],
                                               hgt[rw % 2][:, c*512:(c+1)*512],
                                               start=(c == 0), stop=(c == C16 - 1))
                            ins.then_inc(S["q4_pe"], 1)
                            if g >= 1:
                                t.wait_ge(S["q4_u"], 2 * (g - 1) + 2)
                            for c in range(C16):
                                ins = t.matmul(pk[:, :],
                                               wk_bf[:, c*512 + h*128: c*512 + (h+1)*128],
                                               hgt[rw % 2][:, c*512:(c+1)*512],
                                               start=(c == 0), stop=(c == C16 - 1))
                            ins.then_inc(S["q4_pe"], 1)
                            if g >= 1:
                                stats_bcast(g - 1)
                        for tt in range(4):
                            gv = 4 * rw + tt
                            if gv >= 1:
                                t.wait_ge(S["q4_v"], gv)
                            for c in range(C16):
                                ins = t.matmul(pv[:, :],
                                               hgt[rw % 2][:, c*512 + tt*128: c*512 + (tt+1)*128],
                                               wv_bf[:, c*512:(c+1)*512],
                                               start=(c == 0), stop=(c == C16 - 1))
                            ins.then_inc(S["q4_pe"], 1)
                    stats_bcast(31)

                @blk.scalar
                def _(s):
                    def sqrts(g):
                        for j in range(2):
                            s.wait_ge(S["q4_rowA"], 2 * g + j + 1)
                            s.activation(rows["sd"][:, :],
                                         rows["var"][:, :], AF.Ln, bias=cst[0:1, 0:1])
                            s.drain()
                            s.activation(rows["rr"][:, :], rows["sd"][:, :],
                                         AF.Exp, scale=cst[0:1, 1:2])
                            s.drain().then_inc(S["q4_sd"], 1)
                    for rw in range(8):
                        for h in range(H):
                            g = 4 * rw + h
                            s.wait_ge(S["q4_pe"], 12 * rw + 2 * h + 1)
                            if g >= 4:
                                s.wait_ge(S["q4_qt"], 2 * (g - 4) + 1)
                            s.activation(u_q[g % 4][:, :], pq[:, :], AF.Identity,
                                         scale=gam[:, :], bias=bet[:, :]
                                         ).then_inc(S["q4_u"], 1)
                            s.activation(u2q[g % 4][:, :], u_q[g % 4][:, :], AF.Square)
                            s.drain().then_inc(S["q4_u2"], 1)
                            s.wait_ge(S["q4_pe"], 12 * rw + 2 * h + 2)
                            if g >= 4:
                                s.wait_ge(S["q4_qt"], 2 * (g - 4) + 2)
                            s.activation(u_k[g % 4][:, :], pk[:, :], AF.Identity,
                                         scale=gam[:, :], bias=bet[:, :]
                                         ).then_inc(S["q4_u"], 1)
                            s.activation(u2k[g % 4][:, :], u_k[g % 4][:, :], AF.Square)
                            s.drain().then_inc(S["q4_u2"], 1)
                            if g >= 1:
                                sqrts(g - 1)
                        for tt in range(4):
                            s.wait_ge(S["q4_pe"], 12 * rw + 8 + tt + 1)
                            s.copy(V_sb[:, (4*rw+tt)*512:(4*rw+tt+1)*512], pv[:, :]
                                   ).then_inc(S["q4_v"], 1)
                    sqrts(31)

                @blk.vector
                def _(v):
                    for rw in range(8):
                        for h in range(H):
                            g = 4 * rw + h
                            for j, sfx in enumerate(["q", "k"]):
                                cnt = 2 * g + j
                                v.wait_ge(S["q4_st"], cnt + 1)
                                if cnt >= 1:
                                    v.wait_ge(S["q4_sd"], cnt)
                                v.tensor_single_scalar(rows["mur"][:, :],
                                                       pst1[:, :], 1.0 / 128, ALU.mult)
                                v.tensor_single_scalar(rows["var"][:, :],
                                                       pst2[:, :], 1.0 / 128, ALU.mult
                                                       ).then_inc(S["q4_stf"], 1)
                                v.drain()
                                v.tensor_mul(rows["mrs"][:, :], rows["mur"][:, :],
                                             rows["mur"][:, :])
                                v.drain()
                                v.tensor_sub(rows["var"][:, :], rows["var"][:, :],
                                             rows["mrs"][:, :])
                                v.drain().then_inc(S["q4_rowA"], 1)
                                v.wait_ge(S["q4_sd"], cnt + 1)
                                if sfx == "q":
                                    v.tensor_single_scalar(rows["rs"][:, :],
                                                           rows["rr"][:, :], RD, ALU.mult)
                                    v.drain()
                                    v.tensor_mul(rows["mrs"][:, :], rows["mur"][:, :],
                                                 rows["rs"][:, :])
                                    v.drain()
                                    v.tensor_copy(rowsb["rs"][:, :], rows["rs"][:, :])
                                    v.tensor_copy(rowsb["mrs"][:, :], rows["mrs"][:, :])
                                    v.drain().then_inc(S["q4_rowb"], 1)
                                else:
                                    v.tensor_mul(rows["mrs"][:, :], rows["mur"][:, :],
                                                 rows["rr"][:, :])
                                    v.drain()
                                    v.tensor_copy(rowsb["rs"][:, :], rows["rr"][:, :])
                                    v.tensor_copy(rowsb["mrs"][:, :], rows["mrs"][:, :])
                                    v.drain().then_inc(S["q4_rowb"], 1)
                                v.wait_ge(S["q4_bc"], cnt + 1)
                                usrc = u_q[g % 4] if sfx == "q" else u_k[g % 4]
                                dst = Q_sb if sfx == "q" else K_sb
                                dsl = dst[:, h*SEQ + rw*512: h*SEQ + (rw+1)*512]
                                v.tensor_tensor(dsl, usrc[:, :], pbc1[:, :], ALU.mult)
                                v.tensor_tensor(dsl, dsl, pbc2[:, :], ALU.subtract
                                                ).then_inc(S["q4_qt"], 1)

            if "qkv" in dbg:
                dq = mkdbg("q", [128, H * SEQ]); dk = mkdbg("k", [128, H * SEQ])
                dv = mkdbg("v", [128, H * SEQ])
                with nc.Block() as blk:
                    @blk.gpsimd
                    def _(sy):
                        sy.dma_start(dq.ap(), Q_sb[:, :]).then_inc(S["d_st"], 16)
                        sy.dma_start(dk.ap(), K_sb[:, :]).then_inc(S["d_st"], 16)
                        sy.dma_start(dv.ap(), V_sb[:, :]).then_inc(S["d_st"], 16)
                        sy.wait_ge(S["d_st"], DST[0] + 48)
                DST[0] += 48

        with ExitStack() as ph:
            E = ph.enter_context
            masks = E(nc.sbuf_tensor("masks", [128, 4 * 512], BF16))
            o_sb = E(nc.sbuf_tensor("o_sb", [128, H * SEQ], BF16))
            s2t = [E(nc.sbuf_tensor(f"s2t{i}", [128, 512], BF16)) for i in range(4)]
            zr = E(nc.sbuf_tensor("zr", [128, 512], F32))
            psS = [E(nc.psum_tensor(f"psS{i}", [128, 512], F32)) for i in range(4)]
            psO = [E(nc.psum_tensor(f"psO{i}", [128, 512], F32)) for i in range(2)]
            psZ = [E(nc.psum_tensor(f"psZ{i}", [128, 512], F32)) for i in range(2)]

            sched = [(h, qw, 4 * qw + 4) for h in range(H) for qw in range(8)]

            with nc.Block() as blk:
                @blk.gpsimd
                def _(g):
                    g.dma_start(masks[:, :].rearrange("p (j t) -> p j t", j=4),
                                masksd.ap().rearrange("(j p) t -> p j t", j=4)
                                ).then_inc(S["d_mask"], 16)

                @blk.tensor
                def _(t):
                    gidx = 0
                    nmk = 0
                    for hq, (h, qw, nkt) in enumerate(sched):
                        if hq >= 2:
                            t.wait_ge(S["a_on"], hq - 1)
                        for kt in range(nkt):
                            diag = kt >= 4 * qw
                            if gidx >= 4:
                                t.wait_ge(S["a_sq"], gidx - 3)
                            t.matmul(psS[gidx % 4][:, :],
                                     K_sb[:, h*SEQ + kt*128: h*SEQ + (kt+1)*128],
                                     Q_sb[:, h*SEQ + qw*512: h*SEQ + (qw+1)*512],
                                     start=True, stop=True, skip_group_check=True
                                     ).then_inc(S["a_s"], 1)
                            if diag:
                                nmk += 1
                                t.wait_ge(S["a_mk"], nmk)
                            else:
                                t.wait_ge(S["a_sq"], gidx + 1)
                            t.matmul(psO[hq % 2][:, :],
                                     V_sb[:, kt*512 + h*128: kt*512 + (h+1)*128],
                                     s2t[gidx % 4][:, :],
                                     start=(kt == 0), stop=(kt == nkt - 1),
                                     skip_group_check=True)
                            t.matmul(psZ[hq % 2][:, :], allones[:, :],
                                     s2t[gidx % 4][:, :],
                                     start=(kt == 0), stop=(kt == nkt - 1),
                                     skip_group_check=True).then_inc(S["a_oz"], 1)
                            gidx += 1

                @blk.scalar
                def _(s):
                    gidx = 0
                    for hq, (h, qw, nkt) in enumerate(sched):
                        for kt in range(nkt):
                            s.wait_ge(S["a_s"], gidx + 1)
                            if gidx >= 4:
                                s.wait_ge(S["a_oz"], gidx - 3)
                            s.activation(s2t[gidx % 4][:, :], psS[gidx % 4][:, :],
                                         AF.Square).then_inc(S["a_sq"], 1)
                            gidx += 1

                @blk.vector
                def _(v):
                    v.wait_ge(S["d_mask"], 16)
                    gidx = 0
                    ozc = 0
                    for hq, (h, qw, nkt) in enumerate(sched):
                        for kt in range(nkt):
                            diag = kt >= 4 * qw
                            if diag:
                                j = kt - 4 * qw
                                v.wait_ge(S["a_sq"], gidx + 1)
                                v.tensor_mul(s2t[gidx % 4][:, :], s2t[gidx % 4][:, :],
                                             masks[:, j*512:(j+1)*512]).then_inc(S["a_mk"], 1)
                            gidx += 1
                        ozc += nkt
                        v.wait_ge(S["a_oz"], ozc)
                        v.tensor_single_scalar(zr[:, :], psZ[hq % 2][:, :], EPS, ALU.add)
                        v.reciprocal(zr[:, :], zr[:, :])
                        v.tensor_tensor(o_sb[:, h*SEQ + qw*512: h*SEQ + (qw+1)*512],
                                        psO[hq % 2][:, :], zr[:, :], ALU.mult
                                        ).then_inc(S["a_on"], 1)

            with nc.Block() as blk:
                @blk.sync
                def _(sy):
                    sy.dma_start(
                        od.ap().rearrange("(h p) t -> p h t", h=H),
                        o_sb[:, :].rearrange("p (h t) -> p h t", h=H),
                    ).then_inc(S["d_od"], 16)

                @blk.gpsimd
                def _(g):
                    g.wait_ge(S["d_od"], 16)
                    g.collective_compute(
                        "AllGather", ALU.bypass, replica_groups=GROUPS,
                        ins=[od.ap().opt()], outs=[og.ap().opt()],
                    ).then_inc(S["c_o"], 1)

            if "attn" in dbg:
                do = mkdbg("o", [128, H * SEQ])
                with nc.Block() as blk:
                    @blk.gpsimd
                    def _(sy):
                        sy.dma_start(do.ap(), o_sb[:, :]).then_inc(S["d_st"], 16)
                        sy.wait_ge(S["d_st"], DST[0] + 16)
                DST[0] += 16

        DC[0] += 48
        s2.close()

        h2_sb = R(nc.sbuf_tensor("h2_sb", [128, C16 * T], BF16))
        with ExitStack() as ph:
            E = ph.enter_context
            wo_bf = E(nc.sbuf_tensor("wo_bf", [128, C16 * DM], BF16))
            xb_sb = E(nc.sbuf_tensor("xb_sb", [128, C16 * T], BF16))
            omy = E(nc.sbuf_tensor("omy", [128, C16 * T], BF16))
            xf = [E(nc.sbuf_tensor(f"xf{i}", [128, 512], F32)) for i in range(2)]
            x2t = [E(nc.sbuf_tensor(f"x2t{i}", [128, 512], F32)) for i in range(2)]
            xsq = [E(nc.sbuf_tensor(f"xsq{i}", [128, 512], BF16)) for i in range(2)]
            t7 = [E(nc.sbuf_tensor(f"t7_{i}", [128, 512], BF16)) for i in range(2)]
            rows7 = {nm: E(nc.sbuf_tensor("r7_" + nm, [1, 512], F32))
                     for nm in ["mur", "ex2", "var", "sd", "rr", "mrs"]}
            rows7b = {nm: E(nc.sbuf_tensor("rb7_" + nm, [1, 512], BF16))
                      for nm in ["rs", "mrs"]}
            psW = [E(nc.psum_tensor(f"psW{i}", [128, 512], F32)) for i in range(2)]
            pstA = [E(nc.psum_tensor(f"pstA{i}", [1, 512], F32)) for i in range(2)]
            pstB = [E(nc.psum_tensor(f"pstB{i}", [1, 512], F32)) for i in range(2)]
            pbc7 = [E(nc.psum_tensor(f"pbc7{i}", [128, 512], F32)) for i in range(2)]

            with nc.Block() as blk:
                @blk.gpsimd
                def _(g):
                    g.dma_start(wo_bf[:, :].rearrange("p (c i) -> p c i", c=C16),
                                woT.ap().rearrange("(c p) i -> p c i", c=C16)
                                ).then_inc(S["d_c"], 16)

                @blk.sync
                def _(sy):
                    rnk = sy.cc_rank(GROUPS)
                    moff = rnk * T
                    sy.wait_ge(S["c_o"], 1)
                    sy.dma_start(omy[:, :].rearrange("p (c t) -> p c t", c=C16),
                                 og.ap().rearrange("(c p) t -> p c t", c=C16)
                                   [:, :, bass.ds(moff, T)]
                                 ).then_inc(S["d_omy"], 16)
                    for iw in range(34):
                        if iw >= 2:
                            sy.wait_ge(S["w7_x2"], iw - 1)
                        if iw < 32:
                            win, i = iw // C16, iw % C16
                            sy.dma_start(xf[iw % 2][:, :],
                                         xfmd[i*128:(i+1)*128, win*512:(win+1)*512]
                                         ).then_inc(S["xf%d" % (iw % 2)], 16)
                        if iw >= 2:
                            sw, si = (iw - 2) // C16, (iw - 2) % C16
                            sy.dma_start(x2d[si*128:(si+1)*128, sw*512:(sw+1)*512],
                                         x2t[iw % 2][:, :]).then_inc(S["x2s%d" % (iw % 2)], 16)

                @blk.tensor
                def _(t):
                    t.wait_ge(S["d_omy"], 16)
                    t.wait_ge(S["d_c"], DC[0] + 16)
                    for iw in range(32):
                        win, i = iw // C16, iw % C16
                        if iw >= 2:
                            t.wait_ge(S["w7_x2"], iw - 1)
                        for c in range(C16):
                            ins = t.matmul(psW[iw % 2][:, :],
                                           wo_bf[:, c*DM + i*128: c*DM + (i+1)*128],
                                           omy[:, c*T + win*512: c*T + (win+1)*512],
                                           start=(c == 0), stop=(c == C16 - 1))
                        ins.then_inc(S["w7_pe"], 1)
                        for siw in ([iw - 1] if i >= 1 else []) + ([iw] if i == C16 - 1 else []):
                            swin, si = siw // C16, siw % C16
                            t.wait_ge(S["w7_xb"], siw + 1)
                            t.matmul(pstA[swin][:, :], allones[:, 0:1],
                                     xb_sb[:, si*T + swin*512: si*T + (swin+1)*512],
                                     start=(si == 0), stop=(si == C16 - 1),
                                     skip_group_check=True)
                            t.wait_ge(S["w7_xsq"], siw + 1)
                            t.matmul(pstB[swin][:, :], allones[:, 0:1],
                                     xsq[siw % 2][:, :],
                                     start=(si == 0), stop=(si == C16 - 1),
                                     skip_group_check=True).then_inc(S["w7_st"], 1)

                @blk.vector
                def _(v):
                    for iw in range(32):
                        win, i = iw // C16, iw % C16
                        v.wait_ge(S["w7_pe"], iw + 1)
                        v.wait_ge(S["xf%d" % (iw % 2)], 16 * (iw // 2 + 1))
                        if iw >= 2:
                            v.wait_ge(S["x2s%d" % (iw % 2)], 16 * (iw // 2))
                            v.wait_ge(S["w7_xb"], iw - 1)
                        v.scalar_tensor_tensor(x2t[iw % 2][:, :], psW[iw % 2][:, :],
                                               mga[:, i:i+1], xf[iw % 2][:, :],
                                               ALU.mult, ALU.add).then_inc(S["w7_x2"], 1)

                @blk.scalar
                def _(s):
                    for iw in range(32):
                        win, i = iw // C16, iw % C16
                        s.wait_ge(S["w7_x2"], iw + 1)
                        s.copy(xb_sb[:, i*T + win*512: i*T + (win+1)*512],
                               x2t[iw % 2][:, :])
                        s.drain().then_inc(S["w7_xb"], 1)
                        if iw >= 2:
                            s.wait_ge(S["w7_st"], iw - 1)
                        s.activation(xsq[iw % 2][:, :],
                                     xb_sb[:, i*T + win*512: i*T + (win+1)*512],
                                     AF.Square)
                        s.drain().then_inc(S["w7_xsq"], 1)
            with nc.Block() as blk:
                @blk.vector
                def _(v):
                    for win in range(2):
                        v.tensor_single_scalar(rows7["mur"][:, :], pstA[win][:, :],
                                               1.0 / DM, ALU.mult)
                        v.tensor_single_scalar(rows7["ex2"][:, :], pstB[win][:, :],
                                               1.0 / DM, ALU.mult)
                        v.drain()
                        v.tensor_mul(rows7["var"][:, :], rows7["mur"][:, :],
                                     rows7["mur"][:, :])
                        v.drain()
                        v.tensor_sub(rows7["var"][:, :], rows7["ex2"][:, :],
                                     rows7["var"][:, :])
                        v.drain().then_inc(S["g2_row"], 1)
                        v.wait_ge(S["g2_sd"], win + 1)
                        v.tensor_mul(rows7["mrs"][:, :], rows7["mur"][:, :],
                                     rows7["rr"][:, :])
                        v.drain()
                        v.tensor_copy(rows7b["rs"][:, :], rows7["rr"][:, :])
                        v.tensor_copy(rows7b["mrs"][:, :], rows7["mrs"][:, :])
                        v.drain().then_inc(S["g2_rowb"], 1)
                        v.wait_ge(S["g2_bc"], win + 1)
                        for i in range(C16):
                            iw = win * C16 + i
                            if iw >= 2:
                                v.wait_ge(S["g2_h2"], iw - 1)
                            v.tensor_tensor(t7[iw % 2][:, :],
                                            xb_sb[:, i*T + win*512: i*T + (win+1)*512],
                                            pbc7[0][:, :], ALU.mult)
                            v.tensor_tensor(t7[iw % 2][:, :], t7[iw % 2][:, :],
                                            pbc7[1][:, :], ALU.subtract
                                            ).then_inc(S["g2_t"], 1)

                @blk.scalar
                def _(s):
                    for win in range(2):
                        s.wait_ge(S["g2_row"], win + 1)
                        s.activation(rows7["sd"][:, :], rows7["var"][:, :], AF.Ln,
                                     bias=cst[0:1, 0:1])
                        s.drain()
                        s.activation(rows7["rr"][:, :], rows7["sd"][:, :], AF.Exp,
                                     scale=cst[0:1, 1:2])
                        s.drain().then_inc(S["g2_sd"], 1)
                        for i in range(C16):
                            iw = win * C16 + i
                            s.wait_ge(S["g2_t"], iw + 1)
                            s.activation(h2_sb[:, i*T + win*512: i*T + (win+1)*512],
                                         t7[iw % 2][:, :], AF.Identity,
                                         scale=mcb1[:, i:i+1],
                                         bias=modcols[:, 48 + i:48 + i + 1]
                                         ).then_inc(S["g2_h2"], 1)

                @blk.tensor
                def _(t):
                    for win in range(2):
                        t.wait_ge(S["g2_rowb"], win + 1)
                        if win == 1:
                            t.wait_ge(S["g2_h2"], C16)
                        t.matmul(pbc7[0][:, :], allones[0:1, :], rows7b["rs"][:, :],
                                 start=True, stop=True, skip_group_check=True)
                        t.matmul(pbc7[1][:, :], allones[0:1, :], rows7b["mrs"][:, :],
                                 start=True, stop=True, skip_group_check=True
                                 ).then_inc(S["g2_bc"], 1)

            if "h2" in dbg:
                dh2 = mkdbg("h2", [128, C16 * T]); dx2 = mkdbg("x2", [DM, T])
                with nc.Block() as blk:
                    @blk.gpsimd
                    def _(sy):
                        sy.wait_ge(S["x2s0"], 16 * 16)
                        sy.wait_ge(S["x2s1"], 16 * 16)
                        sy.wait_ge(S["xf0"], 16 * 16)
                        sy.wait_ge(S["xf1"], 16 * 16)
                        sy.dma_start(dh2.ap(), h2_sb[:, :]).then_inc(S["d_st"], 16)
                        sy.dma_start(dx2.ap(), x2d.ap()).then_inc(S["d_st"], 16)
                        sy.wait_ge(S["d_st"], DST[0] + 32)
                DST[0] += 32

        acc = R(nc.sbuf_tensor("acc", [128, C16 * T], F32))
        with ExitStack() as ph:
            E = ph.enter_context
            gw_bf = E(nc.sbuf_tensor("gw_bf", [128, C16 * NEXP], BF16))
            lg = E(nc.sbuf_tensor("lg", [128, 8 * NEXP], F32))
            m1t = E(nc.sbuf_tensor("m1t", [128, 1], F32))
            m2t = E(nc.sbuf_tensor("m2t", [128, 1], F32))
            selt = E(nc.sbuf_tensor("selt", [128, NEXP], F32))
            amt = E(nc.sbuf_tensor("amt", [128, 8 * NEXP], F32))
            sgt = E(nc.sbuf_tensor("sgt", [128, 8 * NEXP], F32))
            a_sb = E(nc.sbuf_tensor("a_sb", [128, 8 * NEXP], F32))
            aT_sb = E(nc.sbuf_tensor("aT_sb", [8, T], F32))
            arow = E(nc.sbuf_tensor("arow", [1, NEXP * T], BF16))
            A_bf = [E(nc.sbuf_tensor(f"A_bf{i}", [128, T], BF16)) for i in range(2)]
            ws = [E(nc.sbuf_tensor(f"ws{i}", [128, 8192], BF16)) for i in range(3)]
            suy = E(nc.sbuf_tensor("suy", [128, 8 * T], BF16))
            tA = [E(nc.sbuf_tensor(f"tA{i}", [128, 512], BF16)) for i in range(2)]
            tA3 = E(nc.sbuf_tensor("tA3", [128, 512], BF16))
            pg = E(nc.psum_tensor("pg", [128, NEXP], F32))
            pt8 = E(nc.psum_tensor("pt8", [8, 128], F32))
            pu = [E(nc.psum_tensor(f"pu{i}", [128, 512], F32)) for i in range(3)]
            pA = [E(nc.psum_tensor(f"pA{i}", [128, 512], F32)) for i in range(2)]

            with nc.Block() as blk:
                @blk.gpsimd
                def _(g):
                    g.dma_start(gw_bf[:, :].rearrange("p (c e) -> p c e", c=C16),
                                gatewT.ap().rearrange("(c p) e -> p c e", c=C16)
                                ).then_inc(S["d_c"], 16)

                @blk.tensor
                def _(t):
                    t.wait_ge(S["d_c"], DC[0] + 32)
                    for tt in range(8):
                        if tt >= 1:
                            t.wait_ge(S["g_ev"], tt)
                        for c in range(C16):
                            ins = t.matmul(pg[:, :],
                                           h2_sb[:, c*T + tt*128: c*T + (tt+1)*128],
                                           gw_bf[:, c*NEXP:(c+1)*NEXP],
                                           start=(c == 0), stop=(c == C16 - 1))
                        ins.then_inc(S["g_pe"], 1)

                @blk.vector
                def _(v):
                    for tt in range(8):
                        v.wait_ge(S["g_pe"], tt + 1)
                        v.tensor_copy(lg[:, tt*NEXP:(tt+1)*NEXP], pg[:, :])
                        v.drain().then_inc(S["g_ev"], 1)
                    for tt in range(8):
                        sl = lg[:, tt*NEXP:(tt+1)*NEXP]
                        v.reduce_max(m1t[:, :], sl, axis=AX)
                        v.drain()
                        v.tensor_single_scalar(selt[:, :], sl, m1t[:, :], ALU.is_ge)
                        v.drain()
                        v.scalar_tensor_tensor(selt[:, :], selt[:, :], -1e30, sl,
                                               ALU.mult, ALU.add)
                        v.drain()
                        v.reduce_max(m2t[:, :], selt[:, :], axis=AX)
                        v.drain()
                        v.tensor_single_scalar(amt[:, tt*NEXP:(tt+1)*NEXP], sl,
                                               m2t[:, :], ALU.is_ge)
                        v.drain()
                    v.wait_ge(S["g_sig"], 1)
                    v.tensor_mul(a_sb[:, :], amt[:, :], sgt[:, :])
                    v.drain().then_inc(S["g_a"], 1)

                @blk.scalar
                def _(s):
                    s.wait_ge(S["g_ev"], 8)
                    s.activation(sgt[:, :], lg[:, :], AF.Sigmoid)
                    s.drain().then_inc(S["g_sig"], 1)

            with nc.Block() as blk:
                @blk.tensor
                def _(t):
                    t.wait_ge(S["g_a"], 1)
                    for tt in range(8):
                        if tt >= 1:
                            t.wait_ge(S["t8_ev"], tt)
                        t.transpose(pt8[:, :], a_sb[:, tt*NEXP:(tt+1)*NEXP], ident[:, :]
                                    ).then_inc(S["t8_tr"], 1)

                @blk.scalar
                def _(s):
                    for tt in range(8):
                        s.wait_ge(S["t8_tr"], tt + 1)
                        s.copy(aT_sb[:, tt*128:(tt+1)*128], pt8[:, :])
                        s.drain().then_inc(S["t8_ev"], 1)

            with nc.Block() as blk:
                @blk.gpsimd
                def _(g):
                    for e in range(NEXP):
                        g.dma_start(arow[0:1, e*T:(e+1)*T], aT_sb[e:e+1, :]
                                    ).then_inc(S["d_ar"], 16)
                    g.wait_ge(S["d_ar"], 16 * NEXP)

            def wsrc(e, j):
                if e == 0:
                    m1s, m3s, m2s = sw1T, sw3T, sw2T
                else:
                    m1s, m3s, m2s = w1T[e - 1], w3T[e - 1], w2T[e - 1]
                if j < 2:
                    return m1s[:, j*512:(j+1)*512].rearrange("(c p) n -> p c n", c=C16)
                if j < 4:
                    return m3s[:, (j-2)*512:(j-2+1)*512].rearrange("(c p) n -> p c n", c=C16)
                dh = j - 4
                return m2s[:, dh*1024:(dh+1)*1024].rearrange("(c p) n -> p c n", c=8)

            NG = 64

            with nc.Block() as blk:
                @blk.gpsimd
                def _(g):
                    gpt = [8, 8, 8, 8, 16, 16]
                    glist = []
                    cum = 0
                    for _e in range(NEXP + 1):
                        for _j in range(6):
                            cum += gpt[_j]
                            glist.append(cum)
                    ld = 0
                    for e in range(NEXP + 1):
                        for j in range(6):
                            if ld >= 3:
                                g.wait_ge(S["e_pu"], glist[ld - 3])
                            view = ws[ld % 3][:, :]
                            if j < 4:
                                dst = view.rearrange("p (c n) -> p c n", c=C16)
                            else:
                                dst = view.rearrange("p (c n) -> p c n", c=8)
                            g.dma_start(dst, wsrc(e, j)).then_inc(S["ww%d" % (ld % 3)], 16)
                            ld += 1

                @blk.tensor
                def _(t):
                    ld = 0
                    grp = 0
                    nyy = 0
                    for e in range(NEXP + 1):
                        if e >= 1:
                            t.wait_ge(S["d_ar"], 16 * NEXP)
                            if e >= 2:
                                t.wait_ge(S["e_Aev"], e - 1)
                            for wn in range(2):
                                ins = t.matmul(pA[wn][:, :], allones[0:1, :],
                                               arow[0:1, (e-1)*T + wn*512:(e-1)*T + (wn+1)*512],
                                               start=True, stop=True, skip_group_check=True)
                            ins.then_inc(S["e_A"], 1)
                        for j in range(6):
                            t.wait_ge(S["ww%d" % (ld % 3)], 16 * (ld // 3 + 1))
                            if j < 4:
                                for m in range(4):
                                    for wn in range(2):
                                        if grp >= 3:
                                            t.wait_ge(S["e_done"], grp - 2)
                                        for c in range(C16):
                                            ins = t.matmul(
                                                pu[grp % 3][:, :],
                                                ws[ld % 3][:, c*512 + m*128: c*512 + (m+1)*128],
                                                h2_sb[:, c*T + wn*512: c*T + (wn+1)*512],
                                                start=(c == 0), stop=(c == C16 - 1))
                                        ins.then_inc(S["e_pu"], 1)
                                        grp += 1
                            else:
                                if j == 4:
                                    t.wait_ge(S["e_yy"], 16 * (e + 1))
                                dh = j - 4
                                for il in range(8):
                                    for wn in range(2):
                                        if grp >= 3:
                                            t.wait_ge(S["e_done"], grp - 2)
                                        for hc in range(8):
                                            ins = t.matmul(
                                                pu[grp % 3][:, :],
                                                ws[ld % 3][:, hc*1024 + il*128: hc*1024 + (il+1)*128],
                                                suy[:, hc*T + wn*512: hc*T + (wn+1)*512],
                                                start=(hc == 0), stop=(hc == 7))
                                        ins.then_inc(S["e_pu"], 1)
                                        grp += 1
                            ld += 1

                @blk.scalar
                def _(s):
                    grp = 0
                    ns = 0
                    for e in range(NEXP + 1):
                        if e >= 1:
                            s.wait_ge(S["e_A"], e)
                            if e >= 3:
                                s.wait_ge(S["e_done"], NG * (e - 2))
                            for wn in range(2):
                                ins = s.copy(A_bf[e % 2][:, wn*512:(wn+1)*512], pA[wn][:, :])
                            ins.then_inc(S["e_Aev"], 1)
                        for j in range(6):
                            if j < 2:
                                for m in range(4):
                                    for wn in range(2):
                                        hc = j * 4 + m
                                        sl = suy[:, hc*T + wn*512: hc*T + (wn+1)*512]
                                        if e == 0:
                                            s.wait_ge(S["e_pu"], grp + 1)
                                            s.activation(sl, pu[grp % 3][:, :], AF.Silu
                                                         ).then_inc(S["e_done"], 1)
                                            s.drain().then_inc(S["e_silu"], 1)
                                        else:
                                            s.wait_ge(S["e_tA"], ns - 16 + 1)
                                            s.activation(sl, tA[ns % 2][:, :], AF.Silu)
                                            s.drain().then_inc(S["e_silu"], 1)
                                        ns += 1
                                        grp += 1
                            elif j < 4:
                                grp += 8
                            else:
                                grp += 16

                @blk.vector
                def _(v):
                    grp = 0
                    ns = 0
                    for e in range(NEXP + 1):
                        for j in range(6):
                            if j < 2:
                                for m in range(4):
                                    for wn in range(2):
                                        if e > 0:
                                            v.wait_ge(S["e_pu"], grp + 1)
                                            v.wait_ge(S["e_Aev"], e)
                                            if ns >= 2:
                                                v.wait_ge(S["e_silu"], ns - 1)
                                            ins = v.tensor_tensor(tA[ns % 2][:, :], pu[grp % 3][:, :],
                                                            A_bf[e % 2][:, wn*512:(wn+1)*512],
                                                            ALU.mult)
                                            ins.then_inc(S["e_tA"], 1)
                                            v.drain().then_inc(S["e_done"], 1)
                                        ns += 1
                                        grp += 1
                            elif j < 4:
                                for m in range(4):
                                    for wn in range(2):
                                        hc = (j - 2) * 4 + m
                                        sl = suy[:, hc*T + wn*512: hc*T + (wn+1)*512]
                                        v.wait_ge(S["e_pu"], grp + 1)
                                        v.wait_ge(S["e_silu"], 16 * e + (j - 2) * 8 + m * 2 + wn + 1)
                                        if e == 0:
                                            v.tensor_copy(tA3[:, :], pu[grp % 3][:, :]
                                                          ).then_inc(S["e_done"], 1)
                                        else:
                                            v.wait_ge(S["e_Aev"], e)
                                            v.tensor_tensor(tA3[:, :], pu[grp % 3][:, :],
                                                            A_bf[e % 2][:, wn*512:(wn+1)*512],
                                                            ALU.mult).then_inc(S["e_done"], 1)
                                        v.drain()
                                        v.tensor_tensor(sl, sl, tA3[:, :], ALU.mult)
                                        v.drain().then_inc(S["e_yy"], 1)
                                        grp += 1
                            else:
                                for il in range(8):
                                    for wn in range(2):
                                        i_g = (j - 4) * 8 + il
                                        v.wait_ge(S["e_pu"], grp + 1)
                                        asl = acc[:, i_g*T + wn*512: i_g*T + (wn+1)*512]
                                        if e == 0:
                                            v.tensor_copy(asl, pu[grp % 3][:, :]
                                                          ).then_inc(S["e_done"], 1)
                                        else:
                                            v.tensor_add(asl, asl, pu[grp % 3][:, :]
                                                         ).then_inc(S["e_done"], 1)
                                        grp += 1

            if "acc" in dbg:
                dasb = mkdbg("asb", [128, 8 * NEXP])
                darow = mkdbg("arow", [1, NEXP * T])
                with nc.Block() as blk:
                    @blk.gpsimd
                    def _(sy):
                        sy.dma_start(dasb.ap(), a_sb[:, :]).then_inc(S["d_st"], 16)
                        sy.dma_start(darow.ap(), arow[0:1, :]).then_inc(S["d_st"], 16)
                        sy.wait_ge(S["d_st"], DST[0] + 32)
                DST[0] += 32
                dacc = mkdbg("acc", [128, C16 * T])
                with nc.Block() as blk:
                    @blk.sync
                    def _(sy):
                        sy.dma_start(dacc.ap(), acc[:, :]).then_inc(S["d_st"], 16)
                        sy.wait_ge(S["d_st"], DST[0] + 16)
                DST[0] += 16

        with ExitStack() as ph:
            E = ph.enter_context
            x2c = [E(nc.sbuf_tensor(f"x2c{i}", [128, T], F32)) for i in range(2)]
            outc = [E(nc.sbuf_tensor(f"outc{i}", [128, T], F32)) for i in range(2)]

            with nc.Block() as blk:
                @blk.sync
                def _(sy):
                    sy.wait_ge(S["x2s0"], 16 * 16)
                    sy.wait_ge(S["x2s1"], 16 * 16)
                    for i in range(C16 + 2):
                        if i >= 2:
                            sy.wait_ge(S["f_dve"], i - 1)
                        if i < C16:
                            sy.dma_start(x2c[i % 2][:, :], x2d[i*128:(i+1)*128, :]
                                         ).then_inc(S["xc%d" % (i % 2)], 16)
                        if i >= 2:
                            si = i - 2
                            sy.dma_start(out[si*128:(si+1)*128, :], outc[si % 2][:, :]
                                         ).then_inc(S["ou%d" % (si % 2)], 16)
                    sy.wait_ge(S["ou0"], 16 * 8)
                    sy.wait_ge(S["ou1"], 16 * 8)

                @blk.vector
                def _(v):
                    for i in range(C16):
                        v.wait_ge(S["xc%d" % (i % 2)], 16 * (i // 2 + 1))
                        if i >= 2:
                            v.wait_ge(S["ou%d" % (i % 2)], 16 * (i // 2))
                        v.scalar_tensor_tensor(outc[i % 2][:, :], acc[:, i*T:(i+1)*T],
                                               mgb[:, i:i+1], x2c[i % 2][:, :],
                                               ALU.mult, ALU.add).then_inc(S["f_dve"], 1)

    return nc, dbg_outs


def prep_inputs(inputs):
    hs = np.asarray(inputs["L0_attn_hidden_states"])
    Wq = np.asarray(inputs["L0_attn_Wq"]); Wk = np.asarray(inputs["L0_attn_Wk"])
    Wv = np.asarray(inputs["L0_attn_Wv"]); Wo = np.asarray(inputs["L0_attn_Wo"])
    gamma = np.asarray(inputs["L0_attn_gamma"]); beta = np.asarray(inputs["L0_attn_beta"])
    gate_w = np.asarray(inputs["L0_ffn_gate_w"])
    w1 = np.asarray(inputs["L0_ffn_w1"]); w2 = np.asarray(inputs["L0_ffn_w2"])
    w3 = np.asarray(inputs["L0_ffn_w3"])
    sw1 = np.asarray(inputs["L0_ffn_sw1"]); sw2 = np.asarray(inputs["L0_ffn_sw2"])
    sw3 = np.asarray(inputs["L0_ffn_sw3"])
    c = np.asarray(inputs["L0_c"]); wmod = np.asarray(inputs["L0_wmod"])

    A = np.ascontiguousarray
    common = dict(
        wmod=A(wmod),
        woT=A(Wo.T),
        gammac=A(gamma.reshape(128, 1)),
        betac=A(beta.reshape(128, 1)),
        gatewT=A(gate_w.T),
        w1T=A(w1.transpose(0, 2, 1)),
        w3T=A(w3.transpose(0, 2, 1)),
        w2T=A(w2.transpose(0, 2, 1)),
        sw1T=A(sw1.T), sw3T=A(sw3.T), sw2T=A(sw2.T),
        identd=np.eye(128, dtype=np.float32),
        allonesd=np.ones((128, 128), dtype=np.float32),
    )
    mk = np.zeros((4, 128, 512), np.float32)
    for j in range(4):
        kk = np.arange(128)[:, None] + j * 128
        qq = np.arange(512)[None, :]
        mk[j] = (kk <= qq).astype(np.float32)
    common["masksd"] = A(mk.reshape(4 * 128, 512))

    in_maps = []
    for core in range(8):
        b, w = core // 4, core % 4
        m = dict(common)
        m["x_tm"] = A(hs[b, w*T:(w+1)*T, :])
        m["cfm"] = A(c[b].reshape(C16, 128).T)
        m["wqT"] = A(Wq.T[:, w*512:(w+1)*512])
        m["wkT"] = A(Wk.T[:, w*512:(w+1)*512])
        m["wvT"] = A(Wv.T[:, w*512:(w+1)*512])
        in_maps.append(m)
    return in_maps


def kernel(**inputs):
    in_maps = prep_inputs(inputs)
    nc, _ = build_program()
    res = run_bass_kernel_spmd(nc, in_maps, core_ids=list(range(8)))
    full = np.empty((2, SEQ, DM), dtype=np.float32)
    for core in range(8):
        b, w = core // 4, core % 4
        full[b, w*T:(w+1)*T, :] = np.asarray(res.results[core]["out"]).T
    return full


# version 04
# speedup vs baseline: 1.0117x; 1.0117x over previous
import numpy as np
from contextlib import ExitStack

import concourse.bass as bass
import concourse.mybir as mybir
from concourse.bass_utils import run_bass_kernel_spmd

F32 = mybir.dt.float32
BF16 = mybir.dt.bfloat16
AF = mybir.ActivationFunctionType
ALU = mybir.AluOpType
AX = mybir.AxisListType.X

T = 1024
SEQ = 4096
DM = 2048
C16 = DM // 128
H = 4
EPS = 1e-5
NEXP = 8
HID = 1024
RD = 0.08838834764831845
MODW = 12288
GROUPS = [[0, 1, 2, 3], [4, 5, 6, 7]]


def build_program(dbg=()):
    nc = bass.Bass("TRN2", target_bir_lowering=False, detect_race_conditions=False)

    x_tm = nc.dram_tensor("x_tm", [T, DM], F32, kind="ExternalInput")
    cfm = nc.dram_tensor("cfm", [128, C16], F32, kind="ExternalInput")
    wmod = nc.dram_tensor("wmod", [DM, MODW], F32, kind="ExternalInput")
    wqT = nc.dram_tensor("wqT", [DM, H * 128], F32, kind="ExternalInput")
    wkT = nc.dram_tensor("wkT", [DM, H * 128], F32, kind="ExternalInput")
    wvT = nc.dram_tensor("wvT", [DM, H * 128], F32, kind="ExternalInput")
    woT = nc.dram_tensor("woT", [DM, DM], F32, kind="ExternalInput")
    gammac = nc.dram_tensor("gammac", [128, 1], F32, kind="ExternalInput")
    betac = nc.dram_tensor("betac", [128, 1], F32, kind="ExternalInput")
    gatewT = nc.dram_tensor("gatewT", [DM, NEXP], F32, kind="ExternalInput")
    w1T = nc.dram_tensor("w1T", [NEXP, DM, HID], F32, kind="ExternalInput")
    w3T = nc.dram_tensor("w3T", [NEXP, DM, HID], F32, kind="ExternalInput")
    w2T = nc.dram_tensor("w2T", [NEXP, HID, DM], F32, kind="ExternalInput")
    sw1T = nc.dram_tensor("sw1T", [DM, HID], F32, kind="ExternalInput")
    sw3T = nc.dram_tensor("sw3T", [DM, HID], F32, kind="ExternalInput")
    sw2T = nc.dram_tensor("sw2T", [HID, DM], F32, kind="ExternalInput")
    identd = nc.dram_tensor("identd", [128, 128], F32, kind="ExternalInput")
    allonesd = nc.dram_tensor("allonesd", [128, 128], F32, kind="ExternalInput")
    masksd = nc.dram_tensor("masksd", [4 * 128, 512], F32, kind="ExternalInput")

    out = nc.dram_tensor("out", [DM, T], F32, kind="ExternalOutput")

    h1d = nc.dram_tensor("h1d", [DM, T // 2], BF16)
    h1db = nc.dram_tensor("h1db", [DM, T // 2], BF16)
    hg = nc.dram_tensor("hg", [4 * DM, T // 2], BF16)
    hgb = nc.dram_tensor("hgb", [4 * DM, T // 2], BF16)
    od = nc.dram_tensor("od", [H * 128, SEQ], BF16)
    og = nc.dram_tensor("og", [DM, SEQ], BF16)
    xfmd = nc.dram_tensor("xfmd", [DM, T], F32)
    x2d = nc.dram_tensor("x2d", [DM, T], F32)

    dbg_outs = {}
    def mkdbg(name, shape, dt=F32):
        dbg_outs[name] = nc.dram_tensor("dbg_" + name, shape, dt, kind="ExternalOutput")
        return dbg_outs[name]

    root = ExitStack()
    with root:
        R = root.enter_context
        ident = R(nc.sbuf_tensor("ident", [128, 128], F32))
        allones = R(nc.sbuf_tensor("allones", [128, 128], BF16))
        gam = R(nc.sbuf_tensor("gam", [128, 1], F32))
        bet = R(nc.sbuf_tensor("bet", [128, 1], F32))
        modcols = R(nc.sbuf_tensor("modcols", [128, 96], F32))
        mca1 = R(nc.sbuf_tensor("mca1", [128, C16], F32))
        mcb1 = R(nc.sbuf_tensor("mcb1", [128, C16], F32))
        mga = R(nc.sbuf_tensor("mga", [128, C16], F32))
        mgb = R(nc.sbuf_tensor("mgb", [128, C16], F32))

        S = {}
        for name in ["mw0", "mw1", "hg0", "hg1", "xt0", "xt1", "xf0", "xf1",
                     "x2s0", "x2s1", "ww0", "ww1", "ww2", "xc0", "xc1",
                     "ou0", "ou1", "d_h1", "d_xfm", "d_od", "d_omy",
                     "d_c", "d_x", "d_hg", "d_st", "c_h", "c_hb", "d_h1b", "h1rdy", "c_o",
                     "m_pe", "m_ev", "m_tr", "m_trev", "m_dve",
                     "b2_sq", "b2_red", "b2_sd", "b2_row", "b2_ln", "b2_tr",
                     "b2_ev", "b2_free",
                     "q4_pe", "q4_u", "q4_u2", "q4_st", "q4_stf", "q4_rowA",
                     "q4_sd", "q4_rowb", "q4_bc", "q4_qt", "q4_v", "q4_free",
                     "a_s", "a_sq", "a_mk", "a_oz", "a_on",
                     "w7_pe", "w7_x2", "w7_xb", "w7_xsq", "w7_st",
                     "g2_row", "g2_sd", "g2_rowb", "g2_bc", "g2_t", "g2_h2",
                     "g_pe", "g_ev", "g_sig", "g_am", "g_a", "t8_tr", "t8_ev",
                     "d_ar", "d_w8", "cl_a", "cl_v", "d_mask",
                     "e_pu", "e_done", "e_silu", "e_tA", "e_yy", "e_w",
                     "e_A", "e_Aev",
                     "d_x2", "f_dve", "d_out"]:
            S[name] = R(nc.semaphore(name))
        DC = [0]; DX = [0]; DST = [0]

        scf = R(nc.sbuf_tensor("scf", [128, 2 * C16], F32))
        cst = R(nc.sbuf_tensor("cst", [128, 2], F32))
        scb = R(nc.sbuf_tensor("scb", [128, C16], BF16))

        with nc.Block() as blk:
            @blk.gpsimd
            def _(g):
                g.dma_start(ident[:, :], identd[:, :]).then_inc(S["d_c"], 16)
                g.dma_start(allones[:, :], allonesd[:, :]).then_inc(S["d_c"], 16)
                g.dma_start(gam[:, :], gammac[:, :]).then_inc(S["d_c"], 16)
                g.dma_start(bet[:, :], betac[:, :]).then_inc(S["d_c"], 16)
                g.dma_start(scf[:, 0:C16], cfm[:, :]).then_inc(S["d_c"], 16)
                g.memset(cst[:, 0:1], EPS)
                g.memset(cst[:, 1:2], -0.5)
                g.wait_ge(S["d_c"], 16 * 5)
        DC[0] = 16 * 5

        NW = MODW // 512
        with ExitStack() as ph:
            E = ph.enter_context
            siluc = E(nc.sbuf_tensor("siluc", [128, C16], F32))
            wmt = [E(nc.sbuf_tensor(f"wmt{i}", [128, C16 * 512], BF16)) for i in range(2)]
            modrow = E(nc.sbuf_tensor("modrow", [1, MODW], F32))
            pmod = [E(nc.psum_tensor(f"pmod{i}", [1, 512], F32)) for i in range(2)]
            ptr1 = [E(nc.psum_tensor(f"ptr1{i}", [128, 1], F32)) for i in range(4)]

            with nc.Block() as blk:
                @blk.scalar
                def _(s):
                    s.activation(siluc[:, :], scf[:, 0:C16], AF.Silu)
                    s.drain().then_inc(S["m_ev"], 1)

                @blk.vector
                def _(v):
                    v.wait_ge(S["m_ev"], 1)
                    v.tensor_copy(scb[:, :], siluc[:, :])
                    v.drain().then_inc(S["m_dve"], 1)

                @blk.gpsimd
                def _(g):
                    for nw in range(NW):
                        if nw >= 2:
                            g.wait_ge(S["m_pe"], nw - 1)
                        g.dma_start(
                            wmt[nw % 2][:, :].rearrange("p (c n) -> p c n", c=C16),
                            wmod.ap().rearrange("(c p) m -> p c m", c=C16)[:, :, nw*512:(nw+1)*512],
                        ).then_inc(S["mw%d" % (nw % 2)], 16)

                @blk.tensor
                def _(t):
                    t.wait_ge(S["m_dve"], 1)
                    for nw in range(NW):
                        t.wait_ge(S["mw%d" % (nw % 2)], 16 * (nw // 2 + 1))
                        if nw >= 2:
                            t.wait_ge(S["m_ev"], nw)
                        for c in range(C16):
                            ins = t.matmul(pmod[nw % 2][:, :], scb[:, c:c+1],
                                           wmt[nw % 2][:, c*512:(c+1)*512],
                                           start=(c == 0), stop=(c == C16 - 1))
                        ins.then_inc(S["m_pe"], 1)

                @blk.scalar
                def _(s):
                    for nw in range(NW):
                        s.wait_ge(S["m_pe"], nw + 1)
                        s.copy(modrow[0:1, nw*512:(nw+1)*512], pmod[nw % 2][:, :]
                               ).then_inc(S["m_ev"], 1)

            with nc.Block() as blk:
                @blk.tensor
                def _(t):
                    for k in range(96):
                        if k >= 4:
                            t.wait_ge(S["m_trev"], k - 3)
                        t.transpose(ptr1[k % 4][:, :], modrow[0:1, k*128:(k+1)*128],
                                    ident[0:1, 0:1]).then_inc(S["m_tr"], 1)

                @blk.scalar
                def _(s):
                    for k in range(96):
                        s.wait_ge(S["m_tr"], k + 1)
                        s.copy(modcols[:, k:k+1], ptr1[k % 4][:, :]).then_inc(S["m_trev"], 1)

            with nc.Block() as blk:
                @blk.scalar
                def _(s):
                    for n, (dst, off) in enumerate([(mga, 32), (mgb, 80)]):
                        if n >= 1:
                            s.wait_ge(S["cl_v"], n)
                        s.activation(scf[:, 0:16], modcols[:, off:off+16], AF.Abs)
                        s.drain().then_inc(S["cl_a"], 1)

                @blk.vector
                def _(v):
                    v.tensor_scalar_add(mca1[:, :], modcols[:, 16:32], 1.0)
                    v.tensor_scalar_add(mcb1[:, :], modcols[:, 64:80], 1.0)
                    for n, (dst, off) in enumerate([(mga, 32), (mgb, 80)]):
                        gsl = modcols[:, off:off+16]
                        v.tensor_single_scalar(dst[:, :], gsl, 0.0, ALU.is_ge)
                        v.drain()
                        v.tensor_scalar(dst[:, :], dst[:, :], 2.0, -1.0, ALU.mult, ALU.add)
                        v.wait_ge(S["cl_a"], n + 1)
                        v.tensor_single_scalar(scf[:, 16:32], scf[:, 0:16], 0.25, ALU.max)
                        v.drain()
                        v.tensor_mul(dst[:, :], dst[:, :], scf[:, 16:32])
                        v.drain().then_inc(S["cl_v"], 1)

            if "mod" in dbg:
                dmod = mkdbg("mod", [128, 96])
                with nc.Block() as blk:
                    @blk.sync
                    def _(sy):
                        sy.dma_start(dmod[:, :], modcols[:, :]).then_inc(S["d_st"], 16)
                        sy.wait_ge(S["d_st"], DST[0] + 16)
                DST[0] += 16

        with ExitStack() as ph:
            E = ph.enter_context
            h1_sb = E(nc.sbuf_tensor("h1_sb", [128, C16 * T], BF16))
            xfm_sb = E(nc.sbuf_tensor("xfm_sb", [128, C16 * T], F32))
            xt = [E(nc.sbuf_tensor(f"xt{i}", [128, DM], F32)) for i in range(2)]
            scr = [E(nc.sbuf_tensor(f"scr{i}", [128, DM], F32)) for i in range(2)]
            sxx = [E(nc.sbuf_tensor(f"sxx{i}", [128, 1], F32)) for i in range(2)]
            sx = [E(nc.sbuf_tensor(f"sx{i}", [128, 1], F32)) for i in range(2)]
            mut = [E(nc.sbuf_tensor(f"mut{i}", [128, 1], F32)) for i in range(2)]
            vart = [E(nc.sbuf_tensor(f"vart{i}", [128, 1], F32)) for i in range(2)]
            sdt = [E(nc.sbuf_tensor(f"sdt{i}", [128, 1], F32)) for i in range(2)]
            rstdt = [E(nc.sbuf_tensor(f"rstdt{i}", [128, 1], F32)) for i in range(2)]
            nmt = [E(nc.sbuf_tensor(f"nmt{i}", [128, 1], F32)) for i in range(2)]
            lnb = [E(nc.sbuf_tensor(f"lnb{i}", [128, DM], F32)) for i in range(2)]
            dbg01 = E(nc.sbuf_tensor("dbg01", [128, 8], F32))
            pt = [E(nc.psum_tensor(f"pt{i}", [128, 128], F32)) for i in range(4)]

            NT = T // 128
            with nc.Block() as blk:
                @blk.sync
                def _(sy):
                    for i in range(NT):
                        if i >= 2:
                            sy.wait_ge(S["b2_ev"], 32 * (i - 1))
                        sy.dma_start(xt[i % 2][:, :], x_tm[i*128:(i+1)*128, :]
                                     ).then_inc(S["xt%d" % (i % 2)], 16)
                        if i == NT - 1:
                            sy.wait_ge(S["h1rdy"], 1)
                            sy.dma_start(h1d.ap().rearrange("(c p) t -> p c t", c=C16),
                                         h1_sb[:, :].rearrange("p (c t) -> p c t", c=C16)
                                           [:, :, 0:T//2]).then_inc(S["d_h1"], 16)

                @blk.scalar
                def _(s):
                    for i in range(NT):
                        s.wait_ge(S["xt%d" % (i % 2)], 16 * (i // 2 + 1))
                        if i >= 2:
                            s.wait_ge(S["b2_red"], 2 * (i - 2) + 1)
                        s.activation(scr[i % 2][:, :], xt[i % 2][:, :], AF.Square
                                     ).then_inc(S["b2_sq"], 1)
                        s.wait_ge(S["b2_red"], 2 * i + 1)
                        s.activation(sdt[i % 2][:, :], vart[i % 2][:, :], AF.Ln,
                                     bias=cst[:, 0:1])
                        s.drain()
                        s.activation(rstdt[i % 2][:, :], sdt[i % 2][:, :], AF.Exp,
                                     scale=cst[:, 1:2])
                        s.drain().then_inc(S["b2_sd"], 1)
                        for k in range(32):
                            s.wait_ge(S["b2_tr"], 32 * i + k + 1)
                            c = k // 2
                            bank = pt[k % 4]
                            if k % 2 == 0:
                                s.copy(xfm_sb[:, c*T + i*128:c*T + (i+1)*128], bank[:, :]
                                       ).then_inc(S["b2_ev"], 1)
                            else:
                                s.activation(h1_sb[:, c*T + i*128:c*T + (i+1)*128],
                                             bank[:, :], AF.Identity,
                                             scale=mca1[:, c:c+1],
                                             bias=modcols[:, c:c+1]
                                             ).then_inc(S["b2_ev"], 1)
                        if i in (3, NT - 1):
                            s.drain().then_inc(S["h1rdy"], 1)

                @blk.vector
                def _(v):
                    for i in range(NT):
                        v.wait_ge(S["xt%d" % (i % 2)], 16 * (i // 2 + 1))
                        if i >= 2:
                            v.wait_ge(S["b2_ln"], i - 1)
                        v.reduce_sum(sx[i % 2][:, :], xt[i % 2][:, :], axis=AX)
                        v.wait_ge(S["b2_sq"], i + 1)
                        v.reduce_sum(sxx[i % 2][:, :], scr[i % 2][:, :], axis=AX)
                        v.drain()
                        v.tensor_single_scalar(mut[i % 2][:, :], sx[i % 2][:, :],
                                               1.0 / DM, ALU.mult)
                        v.tensor_single_scalar(vart[i % 2][:, :], sxx[i % 2][:, :],
                                               1.0 / DM, ALU.mult)
                        v.drain()
                        v.tensor_mul(nmt[i % 2][:, :], mut[i % 2][:, :], mut[i % 2][:, :])
                        v.drain()
                        v.tensor_sub(vart[i % 2][:, :], vart[i % 2][:, :], nmt[i % 2][:, :])
                        v.drain().then_inc(S["b2_red"], 2)
                        v.wait_ge(S["b2_sd"], i + 1)
                        v.tensor_mul(nmt[i % 2][:, :], mut[i % 2][:, :], rstdt[i % 2][:, :])
                        v.drain()
                        v.tensor_single_scalar(nmt[i % 2][:, :], nmt[i % 2][:, :],
                                               -1.0, ALU.mult)
                        v.drain()
                        if dbg and i < 2:
                            v.tensor_copy(dbg01[:, 4*i+0:4*i+1], vart[i % 2][:, :])
                            v.tensor_copy(dbg01[:, 4*i+1:4*i+2], sdt[i % 2][:, :])
                            v.tensor_copy(dbg01[:, 4*i+2:4*i+3], rstdt[i % 2][:, :])
                            v.tensor_copy(dbg01[:, 4*i+3:4*i+4], nmt[i % 2][:, :])
                        v.tensor_scalar(lnb[i % 2][:, :], xt[i % 2][:, :],
                                        rstdt[i % 2][:, :], nmt[i % 2][:, :],
                                        ALU.mult, ALU.add).then_inc(S["b2_ln"], 1)

                @blk.tensor
                def _(t):
                    for i in range(NT):
                        t.wait_ge(S["xt%d" % (i % 2)], 16 * (i // 2 + 1))
                        t.wait_ge(S["b2_ln"], i + 1)
                        for k in range(32):
                            if 32 * i + k >= 4:
                                t.wait_ge(S["b2_ev"], 32 * i + k - 3)
                            c = k // 2
                            src = xt[i % 2] if k % 2 == 0 else lnb[i % 2]
                            t.transpose(pt[k % 4][:, :], src[:, c*128:(c+1)*128],
                                        ident[:, :]).then_inc(S["b2_tr"], 1)
            with nc.Block() as blk:
                @blk.sync
                def _(sy):
                    sy.wait_ge(S["h1rdy"], 2)
                    sy.dma_start(h1db.ap().rearrange("(c p) t -> p c t", c=C16),
                                 h1_sb[:, :].rearrange("p (c t) -> p c t", c=C16)
                                   [:, :, T//2:T]).then_inc(S["d_h1b"], 16)
                    sy.dma_start(xfmd.ap().rearrange("(c p) t -> p c t", c=C16),
                                 xfm_sb[:, :].rearrange("p (c t) -> p c t", c=C16)
                                 ).then_inc(S["d_xfm"], 16)
                    sy.wait_ge(S["d_xfm"], 16)

                @blk.gpsimd
                def _(g):
                    g.wait_ge(S["d_h1"], 16)
                    g.collective_compute(
                        "AllGather", ALU.bypass, replica_groups=GROUPS,
                        ins=[h1d.ap().opt()], outs=[hg.ap().opt()],
                    ).then_inc(S["c_h"], 1)
                    g.wait_ge(S["d_h1b"], 16)
                    g.collective_compute(
                        "AllGather", ALU.bypass, replica_groups=GROUPS,
                        ins=[h1db.ap().opt()], outs=[hgb.ap().opt()],
                    ).then_inc(S["c_hb"], 1)

            if "b2s" in dbg:
                d01 = mkdbg("t01", [128, 8])
                with nc.Block() as blk:
                    @blk.sync
                    def _(sy):
                        sy.dma_start(d01[:, :], dbg01[:, :]).then_inc(S["d_st"], 16)
                        sy.wait_ge(S["d_st"], DST[0] + 16)
                DST[0] += 16
                dst2 = mkdbg("b2s", [128, 12])
                with nc.Block() as blk:
                    @blk.sync
                    def _(sy):
                        for bi, tt in enumerate([sx, sxx, mut, vart, sdt, rstdt]):
                            for par in range(2):
                                with nc.allow_non_contiguous_dma("dbg"):
                                    sy.dma_start(dst2[:, 2*bi+par:2*bi+par+1], tt[par][:, :],
                                                 allow_slow_non_contiguous=True
                                                 ).then_inc(S["d_st"], 16)
                        sy.wait_ge(S["d_st"], DST[0] + 16 * 12)
                DST[0] += 16 * 12

            if "h1" in dbg:
                dh1 = mkdbg("h1", [DM, T]); dxf = mkdbg("xfm", [DM, T])
                with nc.Block() as blk:
                    @blk.gpsimd
                    def _(sy):
                        sy.wait_ge(S["d_h1"], 16)
                        sy.wait_ge(S["d_xfm"], 16)
                        sy.dma_start(dh1.ap(), h1d.ap()).then_inc(S["d_st"], 16)
                        sy.dma_start(dxf.ap(), xfmd.ap()).then_inc(S["d_st"], 16)
                        sy.wait_ge(S["d_st"], DST[0] + 32)
                DST[0] += 32

        s2 = ExitStack()
        E2 = s2.enter_context
        Q_sb = E2(nc.sbuf_tensor("Q_sb", [128, H * SEQ], BF16))
        K_sb = E2(nc.sbuf_tensor("K_sb", [128, H * SEQ], BF16))
        V_sb = E2(nc.sbuf_tensor("V_sb", [128, H * SEQ], BF16))

        RW_ORDER = [0, 2, 4, 6, 1, 3, 5, 7]
        with ExitStack() as ph:
            E = ph.enter_context
            wq_bf = E(nc.sbuf_tensor("wq_bf", [128, C16 * 512], BF16))
            wk_bf = E(nc.sbuf_tensor("wk_bf", [128, C16 * 512], BF16))
            wv_bf = E(nc.sbuf_tensor("wv_bf", [128, C16 * 512], BF16))
            hgt = [E(nc.sbuf_tensor(f"hgt{i}", [128, C16 * 512], BF16)) for i in range(2)]
            u_q = [E(nc.sbuf_tensor(f"u_q{i}", [128, 512], BF16)) for i in range(4)]
            u_k = [E(nc.sbuf_tensor(f"u_k{i}", [128, 512], BF16)) for i in range(4)]
            u2q = [E(nc.sbuf_tensor(f"u2q{i}", [128, 512], BF16)) for i in range(4)]
            u2k = [E(nc.sbuf_tensor(f"u2k{i}", [128, 512], BF16)) for i in range(4)]
            rows = {nm: E(nc.sbuf_tensor("r_" + nm, [1, 512], F32))
                    for nm in ["mur", "var", "sd", "rr", "rs", "mrs"]}
            rowsb = {nm: E(nc.sbuf_tensor("rb_" + nm, [1, 512], BF16))
                    for nm in ["rs", "mrs"]}
            pq = E(nc.psum_tensor("pq", [128, 512], F32))
            pk = E(nc.psum_tensor("pk", [128, 512], F32))
            pv = E(nc.psum_tensor("pv", [128, 512], F32))
            pst1 = E(nc.psum_tensor("pst1", [1, 512], F32))
            pst2 = E(nc.psum_tensor("pst2", [1, 512], F32))
            pbc1 = E(nc.psum_tensor("pbc1", [128, 512], F32))
            pbc2 = E(nc.psum_tensor("pbc2", [128, 512], F32))

            with nc.Block() as blk:
                @blk.gpsimd
                def _(g):
                    for (dst, src) in [(wq_bf, wqT), (wk_bf, wkT), (wv_bf, wvT)]:
                        g.dma_start(dst[:, :].rearrange("p (c i) -> p c i", c=C16),
                                    src.ap().rearrange("(c p) i -> p c i", c=C16)
                                    ).then_inc(S["d_c"], 16)

                @blk.sync
                def _(sy):
                    sy.wait_ge(S["c_h"], 1)
                    for idx, rw in enumerate(RW_ORDER):
                        if idx >= 2:
                            sy.wait_ge(S["q4_pe"], 12 * (idx - 2) + 12)
                        r, win = rw // 2, rw % 2
                        if win == 1:
                            sy.wait_ge(S["c_hb"], 1)
                        src = hg if win == 0 else hgb
                        sy.dma_start(
                            hgt[idx % 2][:, :].rearrange("p (c t) -> p c t", c=C16),
                            src[r*DM:(r+1)*DM, :].rearrange("(c p) t -> p c t", c=C16),
                        ).then_inc(S["hg%d" % (idx % 2)], 16)

                @blk.tensor
                def _(t):
                    def stats_bcast(g):
                        t.wait_ge(S["q4_u2"], 2 * g + 1)
                        if g >= 1:
                            t.wait_ge(S["q4_stf"], 2 * g)
                        t.matmul(pst1[:, :], allones[:, 0:1], u_q[g % 4][:, :],
                                 start=True, stop=True, skip_group_check=True)
                        t.matmul(pst2[:, :], allones[:, 0:1], u2q[g % 4][:, :],
                                 start=True, stop=True, skip_group_check=True
                                 ).then_inc(S["q4_st"], 1)
                        t.wait_ge(S["q4_rowb"], 2 * g + 1)
                        if g >= 1:
                            t.wait_ge(S["q4_qt"], 2 * g)
                        t.matmul(pbc1[:, :], allones[0:1, :], rowsb["rs"][:, :],
                                 start=True, stop=True, skip_group_check=True)
                        t.matmul(pbc2[:, :], allones[0:1, :], rowsb["mrs"][:, :],
                                 start=True, stop=True, skip_group_check=True
                                 ).then_inc(S["q4_bc"], 1)
                        t.wait_ge(S["q4_u2"], 2 * g + 2)
                        t.wait_ge(S["q4_stf"], 2 * g + 1)
                        t.matmul(pst1[:, :], allones[:, 0:1], u_k[g % 4][:, :],
                                 start=True, stop=True, skip_group_check=True)
                        t.matmul(pst2[:, :], allones[:, 0:1], u2k[g % 4][:, :],
                                 start=True, stop=True, skip_group_check=True
                                 ).then_inc(S["q4_st"], 1)
                        t.wait_ge(S["q4_rowb"], 2 * g + 2)
                        t.wait_ge(S["q4_qt"], 2 * g + 1)
                        t.matmul(pbc1[:, :], allones[0:1, :], rowsb["rs"][:, :],
                                 start=True, stop=True, skip_group_check=True)
                        t.matmul(pbc2[:, :], allones[0:1, :], rowsb["mrs"][:, :],
                                 start=True, stop=True, skip_group_check=True
                                 ).then_inc(S["q4_bc"], 1)

                    t.wait_ge(S["d_c"], DC[0] + 48)
                    for idx, rw in enumerate(RW_ORDER):
                        t.wait_ge(S["hg%d" % (idx % 2)], 16 * (idx // 2 + 1))
                        for h in range(H):
                            g = 4 * idx + h
                            if g >= 1:
                                t.wait_ge(S["q4_u"], 2 * (g - 1) + 1)
                            for c in range(C16):
                                ins = t.matmul(pq[:, :],
                                               wq_bf[:, c*512 + h*128: c*512 + (h+1)*128],
                                               hgt[idx % 2][:, c*512:(c+1)*512],
                                               start=(c == 0), stop=(c == C16 - 1))
                            ins.then_inc(S["q4_pe"], 1)
                            if g >= 1:
                                t.wait_ge(S["q4_u"], 2 * (g - 1) + 2)
                            for c in range(C16):
                                ins = t.matmul(pk[:, :],
                                               wk_bf[:, c*512 + h*128: c*512 + (h+1)*128],
                                               hgt[idx % 2][:, c*512:(c+1)*512],
                                               start=(c == 0), stop=(c == C16 - 1))
                            ins.then_inc(S["q4_pe"], 1)
                            if g >= 1:
                                stats_bcast(g - 1)
                        for tt in range(4):
                            gv = 4 * idx + tt
                            if gv >= 1:
                                t.wait_ge(S["q4_v"], gv)
                            for c in range(C16):
                                ins = t.matmul(pv[:, :],
                                               hgt[idx % 2][:, c*512 + tt*128: c*512 + (tt+1)*128],
                                               wv_bf[:, c*512:(c+1)*512],
                                               start=(c == 0), stop=(c == C16 - 1))
                            ins.then_inc(S["q4_pe"], 1)
                    stats_bcast(31)

                @blk.scalar
                def _(s):
                    def sqrts(g):
                        for j in range(2):
                            s.wait_ge(S["q4_rowA"], 2 * g + j + 1)
                            s.activation(rows["sd"][:, :],
                                         rows["var"][:, :], AF.Ln, bias=cst[0:1, 0:1])
                            s.drain()
                            s.activation(rows["rr"][:, :], rows["sd"][:, :],
                                         AF.Exp, scale=cst[0:1, 1:2])
                            s.drain().then_inc(S["q4_sd"], 1)
                    for idx, rw in enumerate(RW_ORDER):
                        for h in range(H):
                            g = 4 * idx + h
                            s.wait_ge(S["q4_pe"], 12 * idx + 2 * h + 1)
                            if g >= 4:
                                s.wait_ge(S["q4_qt"], 2 * (g - 4) + 1)
                            s.activation(u_q[g % 4][:, :], pq[:, :], AF.Identity,
                                         scale=gam[:, :], bias=bet[:, :]
                                         ).then_inc(S["q4_u"], 1)
                            s.activation(u2q[g % 4][:, :], u_q[g % 4][:, :], AF.Square)
                            s.drain().then_inc(S["q4_u2"], 1)
                            s.wait_ge(S["q4_pe"], 12 * idx + 2 * h + 2)
                            if g >= 4:
                                s.wait_ge(S["q4_qt"], 2 * (g - 4) + 2)
                            s.activation(u_k[g % 4][:, :], pk[:, :], AF.Identity,
                                         scale=gam[:, :], bias=bet[:, :]
                                         ).then_inc(S["q4_u"], 1)
                            s.activation(u2k[g % 4][:, :], u_k[g % 4][:, :], AF.Square)
                            s.drain().then_inc(S["q4_u2"], 1)
                            if g >= 1:
                                sqrts(g - 1)
                        for tt in range(4):
                            s.wait_ge(S["q4_pe"], 12 * idx + 8 + tt + 1)
                            s.copy(V_sb[:, (4*rw+tt)*512:(4*rw+tt+1)*512], pv[:, :]
                                   ).then_inc(S["q4_v"], 1)
                    sqrts(31)

                @blk.vector
                def _(v):
                    for idx, rw in enumerate(RW_ORDER):
                        for h in range(H):
                            g = 4 * idx + h
                            for j, sfx in enumerate(["q", "k"]):
                                cnt = 2 * g + j
                                v.wait_ge(S["q4_st"], cnt + 1)
                                if cnt >= 1:
                                    v.wait_ge(S["q4_sd"], cnt)
                                v.tensor_single_scalar(rows["mur"][:, :],
                                                       pst1[:, :], 1.0 / 128, ALU.mult)
                                v.tensor_single_scalar(rows["var"][:, :],
                                                       pst2[:, :], 1.0 / 128, ALU.mult
                                                       ).then_inc(S["q4_stf"], 1)
                                v.drain()
                                v.tensor_mul(rows["mrs"][:, :], rows["mur"][:, :],
                                             rows["mur"][:, :])
                                v.drain()
                                v.tensor_sub(rows["var"][:, :], rows["var"][:, :],
                                             rows["mrs"][:, :])
                                v.drain().then_inc(S["q4_rowA"], 1)
                                v.wait_ge(S["q4_sd"], cnt + 1)
                                if sfx == "q":
                                    v.tensor_single_scalar(rows["rs"][:, :],
                                                           rows["rr"][:, :], RD, ALU.mult)
                                    v.drain()
                                    v.tensor_mul(rows["mrs"][:, :], rows["mur"][:, :],
                                                 rows["rs"][:, :])
                                    v.drain()
                                    v.tensor_copy(rowsb["rs"][:, :], rows["rs"][:, :])
                                    v.tensor_copy(rowsb["mrs"][:, :], rows["mrs"][:, :])
                                    v.drain().then_inc(S["q4_rowb"], 1)
                                else:
                                    v.tensor_mul(rows["mrs"][:, :], rows["mur"][:, :],
                                                 rows["rr"][:, :])
                                    v.drain()
                                    v.tensor_copy(rowsb["rs"][:, :], rows["rr"][:, :])
                                    v.tensor_copy(rowsb["mrs"][:, :], rows["mrs"][:, :])
                                    v.drain().then_inc(S["q4_rowb"], 1)
                                v.wait_ge(S["q4_bc"], cnt + 1)
                                usrc = u_q[g % 4] if sfx == "q" else u_k[g % 4]
                                dst = Q_sb if sfx == "q" else K_sb
                                dsl = dst[:, h*SEQ + rw*512: h*SEQ + (rw+1)*512]
                                v.tensor_tensor(dsl, usrc[:, :], pbc1[:, :], ALU.mult)
                                v.tensor_tensor(dsl, dsl, pbc2[:, :], ALU.subtract
                                                ).then_inc(S["q4_qt"], 1)

            if "qkv" in dbg:
                dq = mkdbg("q", [128, H * SEQ]); dk = mkdbg("k", [128, H * SEQ])
                dv = mkdbg("v", [128, H * SEQ])
                with nc.Block() as blk:
                    @blk.gpsimd
                    def _(sy):
                        sy.dma_start(dq.ap(), Q_sb[:, :]).then_inc(S["d_st"], 16)
                        sy.dma_start(dk.ap(), K_sb[:, :]).then_inc(S["d_st"], 16)
                        sy.dma_start(dv.ap(), V_sb[:, :]).then_inc(S["d_st"], 16)
                        sy.wait_ge(S["d_st"], DST[0] + 48)
                DST[0] += 48

        with ExitStack() as ph:
            E = ph.enter_context
            masks = E(nc.sbuf_tensor("masks", [128, 4 * 512], BF16))
            o_sb = E(nc.sbuf_tensor("o_sb", [128, H * SEQ], BF16))
            s2t = [E(nc.sbuf_tensor(f"s2t{i}", [128, 512], BF16)) for i in range(4)]
            zr = E(nc.sbuf_tensor("zr", [128, 512], F32))
            psS = [E(nc.psum_tensor(f"psS{i}", [128, 512], F32)) for i in range(4)]
            psO = [E(nc.psum_tensor(f"psO{i}", [128, 512], F32)) for i in range(2)]
            psZ = [E(nc.psum_tensor(f"psZ{i}", [128, 512], F32)) for i in range(2)]

            sched = [(h, qw, 4 * qw + 4) for h in range(H) for qw in range(8)]

            with nc.Block() as blk:
                @blk.gpsimd
                def _(g):
                    g.dma_start(masks[:, :].rearrange("p (j t) -> p j t", j=4),
                                masksd.ap().rearrange("(j p) t -> p j t", j=4)
                                ).then_inc(S["d_mask"], 16)

                @blk.tensor
                def _(t):
                    gidx = 0
                    nmk = 0
                    for hq, (h, qw, nkt) in enumerate(sched):
                        if hq >= 2:
                            t.wait_ge(S["a_on"], hq - 1)
                        for kt in range(nkt):
                            diag = kt >= 4 * qw
                            if gidx >= 4:
                                t.wait_ge(S["a_sq"], gidx - 3)
                            t.matmul(psS[gidx % 4][:, :],
                                     K_sb[:, h*SEQ + kt*128: h*SEQ + (kt+1)*128],
                                     Q_sb[:, h*SEQ + qw*512: h*SEQ + (qw+1)*512],
                                     start=True, stop=True, skip_group_check=True
                                     ).then_inc(S["a_s"], 1)
                            if diag:
                                nmk += 1
                                t.wait_ge(S["a_mk"], nmk)
                            else:
                                t.wait_ge(S["a_sq"], gidx + 1)
                            t.matmul(psO[hq % 2][:, :],
                                     V_sb[:, kt*512 + h*128: kt*512 + (h+1)*128],
                                     s2t[gidx % 4][:, :],
                                     start=(kt == 0), stop=(kt == nkt - 1),
                                     skip_group_check=True)
                            t.matmul(psZ[hq % 2][:, :], allones[:, :],
                                     s2t[gidx % 4][:, :],
                                     start=(kt == 0), stop=(kt == nkt - 1),
                                     skip_group_check=True).then_inc(S["a_oz"], 1)
                            gidx += 1

                @blk.scalar
                def _(s):
                    gidx = 0
                    for hq, (h, qw, nkt) in enumerate(sched):
                        for kt in range(nkt):
                            s.wait_ge(S["a_s"], gidx + 1)
                            if gidx >= 4:
                                s.wait_ge(S["a_oz"], gidx - 3)
                            s.activation(s2t[gidx % 4][:, :], psS[gidx % 4][:, :],
                                         AF.Square).then_inc(S["a_sq"], 1)
                            gidx += 1

                @blk.vector
                def _(v):
                    v.wait_ge(S["d_mask"], 16)
                    gidx = 0
                    ozc = 0
                    for hq, (h, qw, nkt) in enumerate(sched):
                        for kt in range(nkt):
                            diag = kt >= 4 * qw
                            if diag:
                                j = kt - 4 * qw
                                v.wait_ge(S["a_sq"], gidx + 1)
                                v.tensor_mul(s2t[gidx % 4][:, :], s2t[gidx % 4][:, :],
                                             masks[:, j*512:(j+1)*512]).then_inc(S["a_mk"], 1)
                            gidx += 1
                        ozc += nkt
                        v.wait_ge(S["a_oz"], ozc)
                        v.tensor_single_scalar(zr[:, :], psZ[hq % 2][:, :], EPS, ALU.add)
                        v.reciprocal(zr[:, :], zr[:, :])
                        v.tensor_tensor(o_sb[:, h*SEQ + qw*512: h*SEQ + (qw+1)*512],
                                        psO[hq % 2][:, :], zr[:, :], ALU.mult
                                        ).then_inc(S["a_on"], 1)

            with nc.Block() as blk:
                @blk.sync
                def _(sy):
                    sy.dma_start(
                        od.ap().rearrange("(h p) t -> p h t", h=H),
                        o_sb[:, :].rearrange("p (h t) -> p h t", h=H),
                    ).then_inc(S["d_od"], 16)

                @blk.gpsimd
                def _(g):
                    g.wait_ge(S["d_od"], 16)
                    g.collective_compute(
                        "AllGather", ALU.bypass, replica_groups=GROUPS,
                        ins=[od.ap().opt()], outs=[og.ap().opt()],
                    ).then_inc(S["c_o"], 1)

            if "attn" in dbg:
                do = mkdbg("o", [128, H * SEQ])
                with nc.Block() as blk:
                    @blk.gpsimd
                    def _(sy):
                        sy.dma_start(do.ap(), o_sb[:, :]).then_inc(S["d_st"], 16)
                        sy.wait_ge(S["d_st"], DST[0] + 16)
                DST[0] += 16

        DC[0] += 48
        s2.close()

        h2_sb = R(nc.sbuf_tensor("h2_sb", [128, C16 * T], BF16))
        with ExitStack() as ph:
            E = ph.enter_context
            wo_bf = E(nc.sbuf_tensor("wo_bf", [128, C16 * DM], BF16))
            xb_sb = E(nc.sbuf_tensor("xb_sb", [128, C16 * T], BF16))
            omy = E(nc.sbuf_tensor("omy", [128, C16 * T], BF16))
            xf = [E(nc.sbuf_tensor(f"xf{i}", [128, 512], F32)) for i in range(2)]
            x2t = [E(nc.sbuf_tensor(f"x2t{i}", [128, 512], F32)) for i in range(2)]
            xsq = [E(nc.sbuf_tensor(f"xsq{i}", [128, 512], BF16)) for i in range(2)]
            t7 = [E(nc.sbuf_tensor(f"t7_{i}", [128, 512], BF16)) for i in range(2)]
            rows7 = {nm: E(nc.sbuf_tensor("r7_" + nm, [1, 512], F32))
                     for nm in ["mur", "ex2", "var", "sd", "rr", "mrs"]}
            rows7b = {nm: E(nc.sbuf_tensor("rb7_" + nm, [1, 512], BF16))
                      for nm in ["rs", "mrs"]}
            psW = [E(nc.psum_tensor(f"psW{i}", [128, 512], F32)) for i in range(2)]
            pstA = [E(nc.psum_tensor(f"pstA{i}", [1, 512], F32)) for i in range(2)]
            pstB = [E(nc.psum_tensor(f"pstB{i}", [1, 512], F32)) for i in range(2)]
            pbc7 = [E(nc.psum_tensor(f"pbc7{i}", [128, 512], F32)) for i in range(2)]

            with nc.Block() as blk:
                @blk.gpsimd
                def _(g):
                    g.dma_start(wo_bf[:, :].rearrange("p (c i) -> p c i", c=C16),
                                woT.ap().rearrange("(c p) i -> p c i", c=C16)
                                ).then_inc(S["d_c"], 16)

                @blk.sync
                def _(sy):
                    rnk = sy.cc_rank(GROUPS)
                    moff = rnk * T
                    sy.wait_ge(S["c_o"], 1)
                    sy.dma_start(omy[:, :].rearrange("p (c t) -> p c t", c=C16),
                                 og.ap().rearrange("(c p) t -> p c t", c=C16)
                                   [:, :, bass.ds(moff, T)]
                                 ).then_inc(S["d_omy"], 16)
                    for iw in range(34):
                        if iw >= 2:
                            sy.wait_ge(S["w7_x2"], iw - 1)
                        if iw < 32:
                            win, i = iw // C16, iw % C16
                            sy.dma_start(xf[iw % 2][:, :],
                                         xfmd[i*128:(i+1)*128, win*512:(win+1)*512]
                                         ).then_inc(S["xf%d" % (iw % 2)], 16)
                        if iw >= 2:
                            sw, si = (iw - 2) // C16, (iw - 2) % C16
                            sy.dma_start(x2d[si*128:(si+1)*128, sw*512:(sw+1)*512],
                                         x2t[iw % 2][:, :]).then_inc(S["x2s%d" % (iw % 2)], 16)

                @blk.tensor
                def _(t):
                    t.wait_ge(S["d_omy"], 16)
                    t.wait_ge(S["d_c"], DC[0] + 16)
                    for iw in range(32):
                        win, i = iw // C16, iw % C16
                        if iw >= 2:
                            t.wait_ge(S["w7_x2"], iw - 1)
                        for c in range(C16):
                            ins = t.matmul(psW[iw % 2][:, :],
                                           wo_bf[:, c*DM + i*128: c*DM + (i+1)*128],
                                           omy[:, c*T + win*512: c*T + (win+1)*512],
                                           start=(c == 0), stop=(c == C16 - 1))
                        ins.then_inc(S["w7_pe"], 1)
                        for siw in ([iw - 1] if i >= 1 else []) + ([iw] if i == C16 - 1 else []):
                            swin, si = siw // C16, siw % C16
                            t.wait_ge(S["w7_xb"], siw + 1)
                            t.matmul(pstA[swin][:, :], allones[:, 0:1],
                                     xb_sb[:, si*T + swin*512: si*T + (swin+1)*512],
                                     start=(si == 0), stop=(si == C16 - 1),
                                     skip_group_check=True)
                            t.wait_ge(S["w7_xsq"], siw + 1)
                            t.matmul(pstB[swin][:, :], allones[:, 0:1],
                                     xsq[siw % 2][:, :],
                                     start=(si == 0), stop=(si == C16 - 1),
                                     skip_group_check=True).then_inc(S["w7_st"], 1)

                @blk.vector
                def _(v):
                    for iw in range(32):
                        win, i = iw // C16, iw % C16
                        v.wait_ge(S["w7_pe"], iw + 1)
                        v.wait_ge(S["xf%d" % (iw % 2)], 16 * (iw // 2 + 1))
                        if iw >= 2:
                            v.wait_ge(S["x2s%d" % (iw % 2)], 16 * (iw // 2))
                            v.wait_ge(S["w7_xb"], iw - 1)
                        v.scalar_tensor_tensor(x2t[iw % 2][:, :], psW[iw % 2][:, :],
                                               mga[:, i:i+1], xf[iw % 2][:, :],
                                               ALU.mult, ALU.add).then_inc(S["w7_x2"], 1)

                @blk.scalar
                def _(s):
                    for iw in range(32):
                        win, i = iw // C16, iw % C16
                        s.wait_ge(S["w7_x2"], iw + 1)
                        s.copy(xb_sb[:, i*T + win*512: i*T + (win+1)*512],
                               x2t[iw % 2][:, :])
                        s.drain().then_inc(S["w7_xb"], 1)
                        if iw >= 2:
                            s.wait_ge(S["w7_st"], iw - 1)
                        s.activation(xsq[iw % 2][:, :],
                                     xb_sb[:, i*T + win*512: i*T + (win+1)*512],
                                     AF.Square)
                        s.drain().then_inc(S["w7_xsq"], 1)
            with nc.Block() as blk:
                @blk.vector
                def _(v):
                    for win in range(2):
                        v.tensor_single_scalar(rows7["mur"][:, :], pstA[win][:, :],
                                               1.0 / DM, ALU.mult)
                        v.tensor_single_scalar(rows7["ex2"][:, :], pstB[win][:, :],
                                               1.0 / DM, ALU.mult)
                        v.drain()
                        v.tensor_mul(rows7["var"][:, :], rows7["mur"][:, :],
                                     rows7["mur"][:, :])
                        v.drain()
                        v.tensor_sub(rows7["var"][:, :], rows7["ex2"][:, :],
                                     rows7["var"][:, :])
                        v.drain().then_inc(S["g2_row"], 1)
                        v.wait_ge(S["g2_sd"], win + 1)
                        v.tensor_mul(rows7["mrs"][:, :], rows7["mur"][:, :],
                                     rows7["rr"][:, :])
                        v.drain()
                        v.tensor_copy(rows7b["rs"][:, :], rows7["rr"][:, :])
                        v.tensor_copy(rows7b["mrs"][:, :], rows7["mrs"][:, :])
                        v.drain().then_inc(S["g2_rowb"], 1)
                        v.wait_ge(S["g2_bc"], win + 1)
                        for i in range(C16):
                            iw = win * C16 + i
                            if iw >= 2:
                                v.wait_ge(S["g2_h2"], iw - 1)
                            v.tensor_tensor(t7[iw % 2][:, :],
                                            xb_sb[:, i*T + win*512: i*T + (win+1)*512],
                                            pbc7[0][:, :], ALU.mult)
                            v.tensor_tensor(t7[iw % 2][:, :], t7[iw % 2][:, :],
                                            pbc7[1][:, :], ALU.subtract
                                            ).then_inc(S["g2_t"], 1)

                @blk.scalar
                def _(s):
                    for win in range(2):
                        s.wait_ge(S["g2_row"], win + 1)
                        s.activation(rows7["sd"][:, :], rows7["var"][:, :], AF.Ln,
                                     bias=cst[0:1, 0:1])
                        s.drain()
                        s.activation(rows7["rr"][:, :], rows7["sd"][:, :], AF.Exp,
                                     scale=cst[0:1, 1:2])
                        s.drain().then_inc(S["g2_sd"], 1)
                        for i in range(C16):
                            iw = win * C16 + i
                            s.wait_ge(S["g2_t"], iw + 1)
                            s.activation(h2_sb[:, i*T + win*512: i*T + (win+1)*512],
                                         t7[iw % 2][:, :], AF.Identity,
                                         scale=mcb1[:, i:i+1],
                                         bias=modcols[:, 48 + i:48 + i + 1]
                                         ).then_inc(S["g2_h2"], 1)

                @blk.tensor
                def _(t):
                    for win in range(2):
                        t.wait_ge(S["g2_rowb"], win + 1)
                        if win == 1:
                            t.wait_ge(S["g2_h2"], C16)
                        t.matmul(pbc7[0][:, :], allones[0:1, :], rows7b["rs"][:, :],
                                 start=True, stop=True, skip_group_check=True)
                        t.matmul(pbc7[1][:, :], allones[0:1, :], rows7b["mrs"][:, :],
                                 start=True, stop=True, skip_group_check=True
                                 ).then_inc(S["g2_bc"], 1)

            if "h2" in dbg:
                dh2 = mkdbg("h2", [128, C16 * T]); dx2 = mkdbg("x2", [DM, T])
                with nc.Block() as blk:
                    @blk.gpsimd
                    def _(sy):
                        sy.wait_ge(S["x2s0"], 16 * 16)
                        sy.wait_ge(S["x2s1"], 16 * 16)
                        sy.wait_ge(S["xf0"], 16 * 16)
                        sy.wait_ge(S["xf1"], 16 * 16)
                        sy.dma_start(dh2.ap(), h2_sb[:, :]).then_inc(S["d_st"], 16)
                        sy.dma_start(dx2.ap(), x2d.ap()).then_inc(S["d_st"], 16)
                        sy.wait_ge(S["d_st"], DST[0] + 32)
                DST[0] += 32

        acc = R(nc.sbuf_tensor("acc", [128, C16 * T], F32))
        with ExitStack() as ph:
            E = ph.enter_context
            gw_bf = E(nc.sbuf_tensor("gw_bf", [128, C16 * NEXP], BF16))
            lg = E(nc.sbuf_tensor("lg", [128, 8 * NEXP], F32))
            m1t = E(nc.sbuf_tensor("m1t", [128, 1], F32))
            m2t = E(nc.sbuf_tensor("m2t", [128, 1], F32))
            selt = E(nc.sbuf_tensor("selt", [128, NEXP], F32))
            amt = E(nc.sbuf_tensor("amt", [128, 8 * NEXP], F32))
            sgt = E(nc.sbuf_tensor("sgt", [128, 8 * NEXP], F32))
            a_sb = E(nc.sbuf_tensor("a_sb", [128, 8 * NEXP], F32))
            aT_sb = E(nc.sbuf_tensor("aT_sb", [8, T], F32))
            arow = E(nc.sbuf_tensor("arow", [1, NEXP * T], BF16))
            A_bf = [E(nc.sbuf_tensor(f"A_bf{i}", [128, T], BF16)) for i in range(2)]
            ws = [E(nc.sbuf_tensor(f"ws{i}", [128, 8192], BF16)) for i in range(3)]
            suy = E(nc.sbuf_tensor("suy", [128, 8 * T], BF16))
            tA = [E(nc.sbuf_tensor(f"tA{i}", [128, 512], BF16)) for i in range(2)]
            tA3 = E(nc.sbuf_tensor("tA3", [128, 512], BF16))
            pg = E(nc.psum_tensor("pg", [128, NEXP], F32))
            pt8 = E(nc.psum_tensor("pt8", [8, 128], F32))
            pu = [E(nc.psum_tensor(f"pu{i}", [128, 512], F32)) for i in range(3)]
            pA = [E(nc.psum_tensor(f"pA{i}", [128, 512], F32)) for i in range(2)]

            with nc.Block() as blk:
                @blk.gpsimd
                def _(g):
                    g.dma_start(gw_bf[:, :].rearrange("p (c e) -> p c e", c=C16),
                                gatewT.ap().rearrange("(c p) e -> p c e", c=C16)
                                ).then_inc(S["d_c"], 16)

                @blk.tensor
                def _(t):
                    t.wait_ge(S["d_c"], DC[0] + 32)
                    for tt in range(8):
                        if tt >= 1:
                            t.wait_ge(S["g_ev"], tt)
                        for c in range(C16):
                            ins = t.matmul(pg[:, :],
                                           h2_sb[:, c*T + tt*128: c*T + (tt+1)*128],
                                           gw_bf[:, c*NEXP:(c+1)*NEXP],
                                           start=(c == 0), stop=(c == C16 - 1))
                        ins.then_inc(S["g_pe"], 1)

                @blk.vector
                def _(v):
                    for tt in range(8):
                        v.wait_ge(S["g_pe"], tt + 1)
                        v.tensor_copy(lg[:, tt*NEXP:(tt+1)*NEXP], pg[:, :])
                        v.drain().then_inc(S["g_ev"], 1)
                    for tt in range(8):
                        sl = lg[:, tt*NEXP:(tt+1)*NEXP]
                        v.reduce_max(m1t[:, :], sl, axis=AX)
                        v.drain()
                        v.tensor_single_scalar(selt[:, :], sl, m1t[:, :], ALU.is_ge)
                        v.drain()
                        v.scalar_tensor_tensor(selt[:, :], selt[:, :], -1e30, sl,
                                               ALU.mult, ALU.add)
                        v.drain()
                        v.reduce_max(m2t[:, :], selt[:, :], axis=AX)
                        v.drain()
                        v.tensor_single_scalar(amt[:, tt*NEXP:(tt+1)*NEXP], sl,
                                               m2t[:, :], ALU.is_ge)
                        v.drain()
                    v.wait_ge(S["g_sig"], 1)
                    v.tensor_mul(a_sb[:, :], amt[:, :], sgt[:, :])
                    v.drain().then_inc(S["g_a"], 1)

                @blk.scalar
                def _(s):
                    s.wait_ge(S["g_ev"], 8)
                    s.activation(sgt[:, :], lg[:, :], AF.Sigmoid)
                    s.drain().then_inc(S["g_sig"], 1)

            with nc.Block() as blk:
                @blk.tensor
                def _(t):
                    t.wait_ge(S["g_a"], 1)
                    for tt in range(8):
                        if tt >= 1:
                            t.wait_ge(S["t8_ev"], tt)
                        t.transpose(pt8[:, :], a_sb[:, tt*NEXP:(tt+1)*NEXP], ident[:, :]
                                    ).then_inc(S["t8_tr"], 1)

                @blk.scalar
                def _(s):
                    for tt in range(8):
                        s.wait_ge(S["t8_tr"], tt + 1)
                        s.copy(aT_sb[:, tt*128:(tt+1)*128], pt8[:, :])
                        s.drain().then_inc(S["t8_ev"], 1)

            with nc.Block() as blk:
                @blk.gpsimd
                def _(g):
                    for e in range(NEXP):
                        g.dma_start(arow[0:1, e*T:(e+1)*T], aT_sb[e:e+1, :]
                                    ).then_inc(S["d_ar"], 16)
                    g.wait_ge(S["d_ar"], 16 * NEXP)

            def wsrc(e, j):
                if e == 0:
                    m1s, m3s, m2s = sw1T, sw3T, sw2T
                else:
                    m1s, m3s, m2s = w1T[e - 1], w3T[e - 1], w2T[e - 1]
                if j < 2:
                    return m1s[:, j*512:(j+1)*512].rearrange("(c p) n -> p c n", c=C16)
                if j < 4:
                    return m3s[:, (j-2)*512:(j-2+1)*512].rearrange("(c p) n -> p c n", c=C16)
                dh = j - 4
                return m2s[:, dh*1024:(dh+1)*1024].rearrange("(c p) n -> p c n", c=8)

            NG = 64

            with nc.Block() as blk:
                @blk.gpsimd
                def _(g):
                    gpt = [8, 8, 8, 8, 16, 16]
                    glist = []
                    cum = 0
                    for _e in range(NEXP + 1):
                        for _j in range(6):
                            cum += gpt[_j]
                            glist.append(cum)
                    ld = 0
                    for e in range(NEXP + 1):
                        for j in range(6):
                            if ld >= 3:
                                g.wait_ge(S["e_pu"], glist[ld - 3])
                            view = ws[ld % 3][:, :]
                            if j < 4:
                                dst = view.rearrange("p (c n) -> p c n", c=C16)
                            else:
                                dst = view.rearrange("p (c n) -> p c n", c=8)
                            g.dma_start(dst, wsrc(e, j)).then_inc(S["ww%d" % (ld % 3)], 16)
                            ld += 1

                @blk.tensor
                def _(t):
                    ld = 0
                    grp = 0
                    nyy = 0
                    for e in range(NEXP + 1):
                        if e >= 1:
                            t.wait_ge(S["d_ar"], 16 * NEXP)
                            if e >= 2:
                                t.wait_ge(S["e_Aev"], e - 1)
                            for wn in range(2):
                                ins = t.matmul(pA[wn][:, :], allones[0:1, :],
                                               arow[0:1, (e-1)*T + wn*512:(e-1)*T + (wn+1)*512],
                                               start=True, stop=True, skip_group_check=True)
                            ins.then_inc(S["e_A"], 1)
                        for j in range(6):
                            t.wait_ge(S["ww%d" % (ld % 3)], 16 * (ld // 3 + 1))
                            if j < 4:
                                for m in range(4):
                                    for wn in range(2):
                                        if grp >= 3:
                                            t.wait_ge(S["e_done"], grp - 2)
                                        for c in range(C16):
                                            ins = t.matmul(
                                                pu[grp % 3][:, :],
                                                ws[ld % 3][:, c*512 + m*128: c*512 + (m+1)*128],
                                                h2_sb[:, c*T + wn*512: c*T + (wn+1)*512],
                                                start=(c == 0), stop=(c == C16 - 1))
                                        ins.then_inc(S["e_pu"], 1)
                                        grp += 1
                            else:
                                if j == 4:
                                    t.wait_ge(S["e_yy"], 16 * (e + 1))
                                dh = j - 4
                                for il in range(8):
                                    for wn in range(2):
                                        if grp >= 3:
                                            t.wait_ge(S["e_done"], grp - 2)
                                        for hc in range(8):
                                            ins = t.matmul(
                                                pu[grp % 3][:, :],
                                                ws[ld % 3][:, hc*1024 + il*128: hc*1024 + (il+1)*128],
                                                suy[:, hc*T + wn*512: hc*T + (wn+1)*512],
                                                start=(hc == 0), stop=(hc == 7))
                                        ins.then_inc(S["e_pu"], 1)
                                        grp += 1
                            ld += 1

                @blk.scalar
                def _(s):
                    grp = 0
                    ns = 0
                    for e in range(NEXP + 1):
                        if e >= 1:
                            s.wait_ge(S["e_A"], e)
                            if e >= 3:
                                s.wait_ge(S["e_done"], NG * (e - 2))
                            for wn in range(2):
                                ins = s.copy(A_bf[e % 2][:, wn*512:(wn+1)*512], pA[wn][:, :])
                            ins.then_inc(S["e_Aev"], 1)
                        for j in range(6):
                            if j < 2:
                                for m in range(4):
                                    for wn in range(2):
                                        hc = j * 4 + m
                                        sl = suy[:, hc*T + wn*512: hc*T + (wn+1)*512]
                                        if e == 0:
                                            s.wait_ge(S["e_pu"], grp + 1)
                                            s.activation(sl, pu[grp % 3][:, :], AF.Silu
                                                         ).then_inc(S["e_done"], 1)
                                            s.drain().then_inc(S["e_silu"], 1)
                                        else:
                                            s.wait_ge(S["e_tA"], ns - 16 + 1)
                                            s.activation(sl, tA[ns % 2][:, :], AF.Silu)
                                            s.drain().then_inc(S["e_silu"], 1)
                                        ns += 1
                                        grp += 1
                            elif j < 4:
                                grp += 8
                            else:
                                grp += 16

                @blk.vector
                def _(v):
                    grp = 0
                    ns = 0
                    for e in range(NEXP + 1):
                        for j in range(6):
                            if j < 2:
                                for m in range(4):
                                    for wn in range(2):
                                        if e > 0:
                                            v.wait_ge(S["e_pu"], grp + 1)
                                            v.wait_ge(S["e_Aev"], e)
                                            if ns >= 2:
                                                v.wait_ge(S["e_silu"], ns - 1)
                                            ins = v.tensor_tensor(tA[ns % 2][:, :], pu[grp % 3][:, :],
                                                            A_bf[e % 2][:, wn*512:(wn+1)*512],
                                                            ALU.mult)
                                            ins.then_inc(S["e_tA"], 1)
                                            v.drain().then_inc(S["e_done"], 1)
                                        ns += 1
                                        grp += 1
                            elif j < 4:
                                for m in range(4):
                                    for wn in range(2):
                                        hc = (j - 2) * 4 + m
                                        sl = suy[:, hc*T + wn*512: hc*T + (wn+1)*512]
                                        v.wait_ge(S["e_pu"], grp + 1)
                                        v.wait_ge(S["e_silu"], 16 * e + (j - 2) * 8 + m * 2 + wn + 1)
                                        if e == 0:
                                            v.tensor_copy(tA3[:, :], pu[grp % 3][:, :]
                                                          ).then_inc(S["e_done"], 1)
                                        else:
                                            v.wait_ge(S["e_Aev"], e)
                                            v.tensor_tensor(tA3[:, :], pu[grp % 3][:, :],
                                                            A_bf[e % 2][:, wn*512:(wn+1)*512],
                                                            ALU.mult).then_inc(S["e_done"], 1)
                                        v.drain()
                                        v.tensor_tensor(sl, sl, tA3[:, :], ALU.mult)
                                        v.drain().then_inc(S["e_yy"], 1)
                                        grp += 1
                            else:
                                for il in range(8):
                                    for wn in range(2):
                                        i_g = (j - 4) * 8 + il
                                        v.wait_ge(S["e_pu"], grp + 1)
                                        asl = acc[:, i_g*T + wn*512: i_g*T + (wn+1)*512]
                                        if e == 0:
                                            v.tensor_copy(asl, pu[grp % 3][:, :]
                                                          ).then_inc(S["e_done"], 1)
                                        else:
                                            v.tensor_add(asl, asl, pu[grp % 3][:, :]
                                                         ).then_inc(S["e_done"], 1)
                                        grp += 1

            if "acc" in dbg:
                dasb = mkdbg("asb", [128, 8 * NEXP])
                darow = mkdbg("arow", [1, NEXP * T])
                with nc.Block() as blk:
                    @blk.gpsimd
                    def _(sy):
                        sy.dma_start(dasb.ap(), a_sb[:, :]).then_inc(S["d_st"], 16)
                        sy.dma_start(darow.ap(), arow[0:1, :]).then_inc(S["d_st"], 16)
                        sy.wait_ge(S["d_st"], DST[0] + 32)
                DST[0] += 32
                dacc = mkdbg("acc", [128, C16 * T])
                with nc.Block() as blk:
                    @blk.sync
                    def _(sy):
                        sy.dma_start(dacc.ap(), acc[:, :]).then_inc(S["d_st"], 16)
                        sy.wait_ge(S["d_st"], DST[0] + 16)
                DST[0] += 16

        with ExitStack() as ph:
            E = ph.enter_context
            x2c = [E(nc.sbuf_tensor(f"x2c{i}", [128, T], F32)) for i in range(2)]
            outc = [E(nc.sbuf_tensor(f"outc{i}", [128, T], F32)) for i in range(2)]

            with nc.Block() as blk:
                @blk.sync
                def _(sy):
                    sy.wait_ge(S["x2s0"], 16 * 16)
                    sy.wait_ge(S["x2s1"], 16 * 16)
                    for i in range(C16 + 2):
                        if i >= 2:
                            sy.wait_ge(S["f_dve"], i - 1)
                        if i < C16:
                            sy.dma_start(x2c[i % 2][:, :], x2d[i*128:(i+1)*128, :]
                                         ).then_inc(S["xc%d" % (i % 2)], 16)
                        if i >= 2:
                            si = i - 2
                            sy.dma_start(out[si*128:(si+1)*128, :], outc[si % 2][:, :]
                                         ).then_inc(S["ou%d" % (si % 2)], 16)
                    sy.wait_ge(S["ou0"], 16 * 8)
                    sy.wait_ge(S["ou1"], 16 * 8)

                @blk.vector
                def _(v):
                    for i in range(C16):
                        v.wait_ge(S["xc%d" % (i % 2)], 16 * (i // 2 + 1))
                        if i >= 2:
                            v.wait_ge(S["ou%d" % (i % 2)], 16 * (i // 2))
                        v.scalar_tensor_tensor(outc[i % 2][:, :], acc[:, i*T:(i+1)*T],
                                               mgb[:, i:i+1], x2c[i % 2][:, :],
                                               ALU.mult, ALU.add).then_inc(S["f_dve"], 1)

    return nc, dbg_outs


def prep_inputs(inputs):
    hs = np.asarray(inputs["L0_attn_hidden_states"])
    Wq = np.asarray(inputs["L0_attn_Wq"]); Wk = np.asarray(inputs["L0_attn_Wk"])
    Wv = np.asarray(inputs["L0_attn_Wv"]); Wo = np.asarray(inputs["L0_attn_Wo"])
    gamma = np.asarray(inputs["L0_attn_gamma"]); beta = np.asarray(inputs["L0_attn_beta"])
    gate_w = np.asarray(inputs["L0_ffn_gate_w"])
    w1 = np.asarray(inputs["L0_ffn_w1"]); w2 = np.asarray(inputs["L0_ffn_w2"])
    w3 = np.asarray(inputs["L0_ffn_w3"])
    sw1 = np.asarray(inputs["L0_ffn_sw1"]); sw2 = np.asarray(inputs["L0_ffn_sw2"])
    sw3 = np.asarray(inputs["L0_ffn_sw3"])
    c = np.asarray(inputs["L0_c"]); wmod = np.asarray(inputs["L0_wmod"])

    A = np.ascontiguousarray
    common = dict(
        wmod=A(wmod),
        woT=A(Wo.T),
        gammac=A(gamma.reshape(128, 1)),
        betac=A(beta.reshape(128, 1)),
        gatewT=A(gate_w.T),
        w1T=A(w1.transpose(0, 2, 1)),
        w3T=A(w3.transpose(0, 2, 1)),
        w2T=A(w2.transpose(0, 2, 1)),
        sw1T=A(sw1.T), sw3T=A(sw3.T), sw2T=A(sw2.T),
        identd=np.eye(128, dtype=np.float32),
        allonesd=np.ones((128, 128), dtype=np.float32),
    )
    mk = np.zeros((4, 128, 512), np.float32)
    for j in range(4):
        kk = np.arange(128)[:, None] + j * 128
        qq = np.arange(512)[None, :]
        mk[j] = (kk <= qq).astype(np.float32)
    common["masksd"] = A(mk.reshape(4 * 128, 512))

    in_maps = []
    for core in range(8):
        b, w = core // 4, core % 4
        m = dict(common)
        m["x_tm"] = A(hs[b, w*T:(w+1)*T, :])
        m["cfm"] = A(c[b].reshape(C16, 128).T)
        m["wqT"] = A(Wq.T[:, w*512:(w+1)*512])
        m["wkT"] = A(Wk.T[:, w*512:(w+1)*512])
        m["wvT"] = A(Wv.T[:, w*512:(w+1)*512])
        in_maps.append(m)
    return in_maps


def kernel(**inputs):
    in_maps = prep_inputs(inputs)
    nc, _ = build_program()
    res = run_bass_kernel_spmd(nc, in_maps, core_ids=list(range(8)))
    full = np.empty((2, SEQ, DM), dtype=np.float32)
    for core in range(8):
        b, w = core // 4, core % 4
        full[b, w*T:(w+1)*T, :] = np.asarray(res.results[core]["out"]).T
    return full
